# version 02
# baseline (speedup 1.0000x reference)
import numpy as np
import concourse.bass as bass
import concourse.mybir as mybir
from concourse.bass_utils import run_bass_kernel_spmd

F32 = mybir.dt.float32
BF16 = mybir.dt.bfloat16
AF = mybir.ActivationFunctionType
ALU = mybir.AluOpType
AX = mybir.AxisListType

P = 128
L = 2048
DM = 2048
E = 4096
NST = 16
RK = 128
IT = 5632
TC = 512
NCH = L // TC
NET = E // P
NKD = DM // P
NDT = DM // P
NIT = IT // P
EPS = 1e-5
SKIP = set()
STRICT = set()


def _host_prep(inp):
    c = np.ascontiguousarray
    w = {}
    ipw = inp["L0_attn_in_proj_w"].astype(np.float32)
    w["winx"] = c(ipw[:E].reshape(NET, P, NKD, P).transpose(0, 3, 2, 1))
    w["winz"] = c(ipw[E:].reshape(NET, P, NKD, P).transpose(0, 3, 2, 1))
    xpw = inp["L0_attn_x_proj_w"].astype(np.float32)
    w["wxp1"] = c(xpw[:RK].T.reshape(NET, P, RK).transpose(1, 0, 2))
    w["wxp2"] = c(xpw[RK:].T.reshape(NET, P, 32).transpose(1, 0, 2))
    dtw = inp["L0_attn_dt_proj_w"].astype(np.float32)
    w["wdt"] = c(dtw.T.reshape(RK, NET, P))
    opw = inp["L0_attn_out_proj_w"].astype(np.float32)
    w["wout"] = c(opw.reshape(NDT, P, NET, P).transpose(0, 3, 2, 1))
    gw = inp["L0_ffn_gate_w"].astype(np.float32)
    w["wg"] = c(gw.reshape(NIT, P, NKD, P).transpose(0, 3, 2, 1))
    uw = inp["L0_ffn_up_w"].astype(np.float32)
    w["wu"] = c(uw.reshape(NIT, P, NKD, P).transpose(0, 3, 2, 1))
    dw = inp["L0_ffn_down_w"].astype(np.float32)
    w["wd"] = c(dw.reshape(NDT, P, NIT, P).transpose(0, 3, 2, 1))
    v32 = lambda v: c(v.astype(np.float32).reshape(-1, P).T)
    w["convw"] = c(inp["L0_attn_conv_w"].astype(np.float32)
                   .reshape(NET, P, 4).transpose(1, 0, 2).reshape(P, NET * 4))
    w["convb"] = v32(inp["L0_attn_conv_b"])
    w["dtb"] = v32(inp["L0_attn_dt_proj_b"])
    w["dvec"] = v32(inp["L0_attn_D"])
    w["alog"] = c(inp["L0_attn_A_log"].astype(np.float32)
                  .reshape(NET, P, NST).transpose(1, 0, 2).reshape(P, NET * NST))
    w["gs"] = v32(inp["L0_ffn_gate_s"])
    w["us"] = v32(inp["L0_ffn_up_s"])
    w["ds"] = v32(inp["L0_ffn_down_s"])
    w["ident"] = np.eye(P, dtype=np.float32)
    return w


def _host_x(x):
    return np.ascontiguousarray(
        x.astype(np.float32).reshape(NCH, TC, NKD, P).transpose(0, 2, 3, 1))


def build_program(debug=False, nch=NCH, phases="ABCD"):
    nc = bass.Bass("TRN2", target_bir_lowering=False)
    dt_ = nc.dram_tensor
    d_in = {}
    for name, shape in [
        ("winx", [NET, P, NKD, P]), ("winz", [NET, P, NKD, P]),
        ("wxp1", [P, NET, RK]), ("wxp2", [P, NET, 32]), ("wdt", [RK, NET, P]),
        ("wout", [NDT, P, NET, P]), ("wg", [NIT, P, NKD, P]),
        ("wu", [NIT, P, NKD, P]), ("wd", [NDT, P, NIT, P]),
        ("convw", [P, NET * 4]), ("convb", [P, NET]), ("dtb", [P, NET]),
        ("dvec", [P, NET]), ("alog", [P, NET * NST]),
        ("gs", [P, NIT]), ("us", [P, NIT]), ("ds", [P, NDT]),
        ("ident", [P, P]), ("xt_host", [NCH, NKD, P, TC]),
    ]:
        d_in[name] = dt_(name, shape, F32, kind="ExternalInput").ap()
    out_d = dt_("out", [L, DM], F32, kind="ExternalOutput").ap()
    xt_scr = dt_("xt_scr", [NET, P, TC], F32, kind="Internal").ap()
    bc_scr = dt_("bc_scr", [32, TC], F32, kind="Internal").ap()
    dbg = {}
    if debug:
        for name, shape in [("dbg_xt", [NCH, P, TC]), ("dbg_dt", [NCH, P, TC]),
                            ("dbg_yred", [NCH, P, TC]), ("dbg_du", [NCH, P, TC]),
                            ("dbg_x1", [NCH, P, TC]), ("dbg_proj", [NCH, 32, TC]),
                            ("dbg_y", [NCH, P, TC]), ("dbg_sg", [NCH, P, TC]),
                            ("dbg_hprev", [NCH, P, 512])]:
            dbg[name] = dt_(name, shape, F32, kind="ExternalOutput").ap()

    from contextlib import ExitStack
    es = ExitStack()
    sb = lambda n, s, d: es.enter_context(nc.sbuf_tensor(n, s, d))
    ps = lambda n, s: es.enter_context(nc.psum_tensor(n, s, F32))
    with es:
        c_convw = sb("c_convw", [P, NET * 4], F32)
        c_convb = sb("c_convb", [P, NET], F32)
        c_dtb = sb("c_dtb", [P, NET], F32)
        c_D = sb("c_D", [P, NET], F32)
        c_A = sb("c_A", [P, NET * NST], F32)
        c_gs = sb("c_gs", [P, NIT], F32)
        c_us = sb("c_us", [P, NIT], F32)
        c_ds = sb("c_ds", [P, NDT], F32)
        ones_bf = sb("ones_bf", [P, 1], BF16)
        ones_row = sb("ones_row", [1, P], F32)
        c_row4 = sb("c_row4", [1, 4], F32)
        c_id = sb("c_id", [P, P], F32)
        u8 = sb("u8", [P, NET * P], BF16)
        w_xp2 = sb("w_xp2", [P, NET * 32], BF16)
        hist = sb("hist", [P, NET * 3], F32)
        hprev = sb("hprev", [P, NET * NST], F32)
        cb0 = sb("cb0", [P, TC + 4], F32)
        xt0 = sb("xt0", [P, TC], F32)
        xt1 = sb("xt1", [P, TC], F32)
        xtb0 = sb("xtb0", [P, TC], BF16)
        xtb1 = sb("xtb1", [P, TC], BF16)
        dtlow = sb("dtlow", [P, TC], BF16)
        bc32 = sb("bc32", [32, TC], F32)
        dt_f = sb("dt_f", [P, TC], F32)
        du = sb("du", [P, TC], F32)
        t1 = sb("t1", [P, TC], F32)
        t2 = sb("t2", [P, TC], F32)
        dA0 = sb("dA0", [P, TC], F32)
        dA1 = sb("dA1", [P, TC], F32)
        wb0 = sb("wb0", [P, TC], F32)
        wb1 = sb("wb1", [P, TC], F32)
        yred = sb("yred", [P, TC], F32)
        ypre = sb("ypre", [P, TC], F32)
        wsA = sb("wsA", [P, 4096], BF16)
        wsB = sb("wsB", [P, 4096], BF16)
        xbf = sb("xbf", [P, NKD * TC], BF16)
        ar1 = sb("ar1", [P, 16384], F32)
        ar2 = sb("ar2", [P, 16384], F32)

        Bnm = ar1[:, 0:8192]
        Cnm = ar1[:, 8192:16384]
        x1 = ar1[:, 0:8192]
        x1b = ar1[:, 8192:12288].bitcast(BF16)
        ob0 = ar1[:, 12288:14336]
        ob1 = ar1[:, 14336:16384]
        htn = ar2[:, 0:8192]
        ybf = ar2[:, 8192:16384].bitcast(BF16)
        x_res = ar2[:, 0:8192]
        hdn = ar2[:, 0:11264].bitcast(BF16)
        ob2 = ar2[:, 11264:13312]
        ob3 = ar2[:, 13312:15360]
        obufs = [ob0, ob1, ob2, ob3]
        srow_a = du[0:1, :]
        srow_b = yred[0:1, :]
        trow = t1[0:1, :]

        pb = [ps(f"pb{i}", [P, TC]) for i in range(8)]

        sG = es.enter_context(nc.semaphore("sG"))
        sS = es.enter_context(nc.semaphore("sS"))
        sT = es.enter_context(nc.semaphore("sT"))
        sV = es.enter_context(nc.semaphore("sV"))
        sA = es.enter_context(nc.semaphore("sA"))
        sE = es.enter_context(nc.semaphore("sE"))
        sGx = es.enter_context(nc.semaphore("sGx"))
        sGu = es.enter_context(nc.semaphore("sGu"))
        sGb = es.enter_context(nc.semaphore("sGb"))
        sSl = es.enter_context(nc.semaphore("sSl"))
        sSx = es.enter_context(nc.semaphore("sSx"))
        sGs = [es.enter_context(nc.semaphore(f"sGs{i}")) for i in range(4)]
        sSt = [es.enter_context(nc.semaphore(f"sSt{i}")) for i in range(2)]
        sLd = [es.enter_context(nc.semaphore(f"sLd{i}")) for i in range(2)]
        sXr = [es.enter_context(nc.semaphore(f"sXr{i}")) for i in range(2)]
        sems = dict(sG=sG, sS=sS, sT=sT, sV=sV, sA=sA, sE=sE, sGx=sGx,
                    sGu=sGu, sGb=sGb, sSl=sSl, sSx=sSx)
        for _i in range(4):
            sems[f"sGs{_i}"] = sGs[_i]
        for _i in range(2):
            sems[f"sSt{_i}"] = sSt[_i]
            sems[f"sLd{_i}"] = sLd[_i]
            sems[f"sXr{_i}"] = sXr[_i]
        blk = es.enter_context(nc.Block())

        n = dict(sG=0, sS=0, sT=0, sV=0, sA=0, sE=0, sGx=0, sGu=0,
                 sGb=0, sSl=0, sSx=0,
                 sGs0=0, sGs1=0, sGs2=0, sGs3=0,
                 sSt0=0, sSt1=0, sLd0=0, sLd1=0, sXr0=0, sXr1=0)

        def barrier():
            snap = dict(n)
            allsems = tuple(sems.keys())
            for dec, eng_sems in [
                (blk.sync, allsems),
                (blk.gpsimd, allsems),
                (blk.tensor, allsems),
                (blk.vector, allsems),
                (blk.scalar, allsems),
            ]:
                def _b(eng, names=eng_sems):
                    for nm in names:
                        if snap[nm]:
                            eng.wait_ge(sems[nm], snap[nm])
                dec(_b)

        @blk.sync
        def _(sync):
            for dst, src in [(c_convw, d_in["convw"]), (c_convb, d_in["convb"]),
                             (c_dtb, d_in["dtb"]), (c_D, d_in["dvec"]),
                             (c_A, d_in["alog"]), (c_gs, d_in["gs"]),
                             (c_us, d_in["us"]), (c_ds, d_in["ds"]),
                             (c_id, d_in["ident"])]:
                sync.dma_start(out=dst[:], in_=src).then_inc(sS, 16)
                n["sS"] += 16

        @blk.scalar
        def _(sc):
            sc.wait_ge(sS, n["sS"])
            sc.activation(c_A[:], c_A[:], AF.Exp).then_inc(sA, 1)
            n["sA"] += 1

        @blk.vector
        def _(v):
            v.memset(ones_row[:], 1.0)
            v.memset(c_row4[0:1, 0:1], EPS)
            v.memset(c_row4[0:1, 1:2], -0.5)
            v.memset(c_row4[0:1, 2:3], 1.0 / DM)
            v.memset(ones_bf[:], 1.0)
            v.memset(hprev[:], 0.0)
            v.memset(hist[:], 0.0)
            v.wait_ge(sA, n["sA"])
            v.tensor_scalar(c_A[:], c_A[:], -1.0, None, ALU.mult).then_inc(sV, 1)
            n["sV"] += 1

        barrier()

        dbg_marks = {}

        for ch in range(nch):
            phA = "A" in phases
            Ag = phA or "g" in phases
            As = phA or "s" in phases
            At = phA or "t" in phases
            Aa = phA or "a" in phases
            Av = phA or "v" in phases
            bG, bS, bT, bV, bA = n["sG"], n["sS"], n["sT"], n["sV"], n["sA"]
            bE = n["sE"]
            bGx, bGu, bGb = n["sGx"], n["sGu"], n["sGb"]
            bSl, bSx = n["sSl"], n["sSx"]
            bGs = [n[f"sGs{i}"] for i in range(4)]
            bSt = [n[f"sSt{i}"] for i in range(2)]
            G_xbf = bGx + 16
            G_xp1 = bGu + 32
            G_slab = lambda et: (et % 2, bGs[et % 2] + 16 * (et // 2 + 1))
            S_st = lambda et: (et % 2, bSt[et % 2] + 16 * (et // 2 + 1))
            S_bc = bS + 16
            T_ip = lambda et: bT + 3 * et + 1
            T_xp = lambda et: bT + 3 * et + 3
            A_ev = lambda et: bA + 3 * et + 1
            A_si = lambda et: bA + 3 * et + 2
            A_ct = lambda et: bA + 3 * et + 3
            V_cv = lambda et: bV + et + 1

            @blk.gpsimd
            def _(g, ch=ch, bG=bG):
                if not Ag: return
                g.dma_start(out=xbf[:].rearrange("p (k t) -> p k t", k=NKD),
                            in_=d_in["xt_host"][ch].rearrange("k p t -> p k t")
                            ).then_inc(sGx, 16)
                g.dma_start(out=u8[:].rearrange("p (e m) -> p e m", e=NET),
                            in_=d_in["wxp1"]).then_inc(sGu, 16)
                g.dma_start(out=w_xp2[:].rearrange("p (e m) -> p e m", e=NET),
                            in_=d_in["wxp2"]).then_inc(sGu, 16)
                for et in range(NET):
                    if "g1" in STRICT and et >= 1:
                        g.wait_ge(sT, T_xp(et - 1))
                    if et >= 2:
                        g.wait_ge(sT, T_ip(et - 2))
                    wsl = (wsA if et % 2 == 0 else wsB)[:, 0:NKD * P]
                    g.dma_start(out=wsl.rearrange("p (k m) -> p k m", k=NKD),
                                in_=d_in["winx"][et]).then_inc(sGs[et % 2], 16)
                n["sGs0"] += 16 * (NET // 2)
                n["sGs1"] += 16 * (NET // 2)
                n["sGx"] += 16
                n["sGu"] += 32

            @blk.sync
            def _(sync, ch=ch):
                if not As: return
                for et in range(NET):
                    sync.wait_ge(sA, A_si(et))
                    xt = xt0 if et % 2 == 0 else xt1
                    sync.dma_start(out=xt_scr[et], in_=xt[:]
                                   ).then_inc(sSt[et % 2], 16)
                sync.wait_ge(sV, bV + NET + 1)
                sync.dma_start(out=bc_scr, in_=bc32[:]).then_inc(sS, 16)
                n["sSt0"] += 16 * (NET // 2)
                n["sSt1"] += 16 * (NET // 2)
                n["sS"] += 16

            @blk.tensor
            def _(t, ch=ch):
                if not At: return
                t.wait_ge(sGx, G_xbf)
                t.wait_ge(sGu, G_xp1)
                for et in range(NET):
                    sl, v_ = G_slab(et)
                    t.wait_ge(sGs[sl], v_)
                    if "t1" in STRICT and et >= 1:
                        t.wait_ge(sA, A_ev(et - 1))
                        t.wait_ge(sV, bV + et)
                    if et >= 2:
                        t.wait_ge(sA, A_ev(et - 2))
                    wsl = (wsA if et % 2 == 0 else wsB)[:, 0:NKD * P]
                    px = pb[et % 2]
                    if "ip" not in SKIP:
                        for k in range(NKD):
                            mm = t.matmul(px[:], wsl[:, k * P:(k + 1) * P],
                                          xbf[:, k * TC:(k + 1) * TC],
                                          start=(k == 0), stop=(k == NKD - 1))
                        mm.then_inc(sT, 1)
                    else:
                        t.nop().then_inc(sT, 1)
                    t.wait_ge(sA, A_ct(et))
                    xb = xtb0 if et % 2 == 0 else xtb1
                    if "xp" not in SKIP:
                        t.matmul(pb[2][:], u8[:, et * P:(et + 1) * P], xb[:],
                                 start=(et == 0), stop=(et == NET - 1)).then_inc(sT, 1)
                        t.matmul(pb[3][0:32, :], w_xp2[:, et * 32:(et + 1) * 32], xb[:],
                                 start=(et == 0), stop=(et == NET - 1)).then_inc(sT, 1)
                    else:
                        t.nop().then_inc(sT, 1)
                        t.nop().then_inc(sT, 1)
                t.matmul(pb[6][:, 0:P], ones_row[:], ones_row[:],
                         start=True, stop=True).then_inc(sT, 1)
                n["sT"] += 3 * NET + 1

            @blk.scalar
            def _(sc, ch=ch):
                if not Aa: return
                for et in range(NET):
                    sc.wait_ge(sT, T_ip(et))
                    if et >= 1:
                        sc.wait_ge(sV, V_cv(et - 1))
                        sc.wait_ge(sE, bE + 2 * et)
                        if "a1" in STRICT:
                            sc.wait_ge(sS, S_st(et - 1))
                    if et >= 2:
                        _sl, _v = S_st(et - 2)
                        sc.wait_ge(sSt[_sl], _v)
                    cb = cb0
                    px = pb[et % 2]
                    if "ev" not in SKIP:
                        sc.activation(cb[:, 3:3 + TC], px[:], AF.Copy).then_inc(sA, 1)
                    else:
                        sc.nop().then_inc(sA, 1)
                    sc.wait_ge(sV, V_cv(et))
                    xt = xt0 if et % 2 == 0 else xt1
                    if "si" not in SKIP:
                        sc.activation(xt[:], t2[:], AF.Silu,
                                      bias=c_convb[:, et:et + 1]).then_inc(sA, 1)
                    else:
                        sc.nop().then_inc(sA, 1)
                    if et >= 1:
                        sc.wait_ge(sT, T_xp(et - 1))
                    xb = xtb0 if et % 2 == 0 else xtb1
                    if "ct" not in SKIP:
                        sc.activation(xb[:], xt[:], AF.Copy).then_inc(sA, 1)
                    else:
                        sc.nop().then_inc(sA, 1)
                n["sA"] += 3 * NET
                sc.wait_ge(sT, bT + 3 * NET + 1)
                sc.activation(dtlow[:], pb[2][:], AF.Copy).then_inc(sA, 1)
                n["sA"] += 1

            @blk.vector
            def _(v, ch=ch):
                if not Av: return
                for et in range(NET):
                    cb = cb0
                    if "hc" not in SKIP:
                        v.tensor_copy(cb[:, 0:3], hist[:, et * 3:(et + 1) * 3]
                                      ).then_inc(sE, 1)
                        v.wait_ge(sE, bE + 2 * et + 1)
                    v.wait_ge(sA, A_ev(et))
                    if "cv" not in SKIP:
                        v.tensor_scalar(t1[:], cb[:, 0:TC],
                                        c_convw[:, et * 4:et * 4 + 1], None, ALU.mult)
                        v.scalar_tensor_tensor(t2[:], cb[:, 1:1 + TC],
                                               c_convw[:, et * 4 + 1:et * 4 + 2],
                                               t1[:], ALU.mult, ALU.add)
                        v.scalar_tensor_tensor(t1[:], cb[:, 2:2 + TC],
                                               c_convw[:, et * 4 + 2:et * 4 + 3],
                                               t2[:], ALU.mult, ALU.add)
                        v.scalar_tensor_tensor(t2[:], cb[:, 3:3 + TC],
                                               c_convw[:, et * 4 + 3:et * 4 + 4],
                                               t1[:], ALU.mult, ALU.add).then_inc(sV, 1)
                    else:
                        v.nop().then_inc(sV, 1)
                    if "hc" not in SKIP:
                        v.tensor_copy(hist[:, et * 3:(et + 1) * 3], cb[:, TC:TC + 3]
                                      ).then_inc(sE, 1)
                    n["sE"] += 2
                n["sV"] += NET
                v.wait_ge(sT, bT + 3 * NET + 1)
                v.tensor_copy(bc32[:], pb[3][0:32, :]).then_inc(sV, 1)
                n["sV"] += 1

            barrier()

            phB = "B" in phases
            bG, bS, bT, bV, bA = n["sG"], n["sS"], n["sT"], n["sV"], n["sA"]
            bGx, bGu, bGb = n["sGx"], n["sGu"], n["sGb"]
            bSl, bSx = n["sSl"], n["sSx"]
            bGs = [n[f"sGs{i}"] for i in range(4)]
            bLd = [n[f"sLd{i}"] for i in range(2)]
            G_wdt = bGu + 16
            G_cnm = bGb + 32
            G_zsl = lambda et: (et % 2, bGs[et % 2] + 16 * (et // 2 + 1))
            S_xt = lambda et: (et % 2, bLd[et % 2] + 16 * (et // 2 + 1))
            T_dt = lambda et: bT + 2 * et + 1
            T_z = lambda et: bT + 2 * et + 2
            A_sp = lambda et: bA + 19 * et + 1
            A_zr = lambda et: bA + 19 * et + 2
            A_sg = lambda et: bA + 19 * et + 3
            A_dA = lambda et, nn: bA + 19 * et + 4 + nn
            V_dtf = lambda et: bV + 19 * et + 1
            V_zsp = lambda et: bV + 19 * et + 2
            V_sc = lambda et, nn: bV + 19 * et + 3 + nn
            V_y = lambda et: bV + 19 * et + 19

            @blk.gpsimd
            def _(g, ch=ch):
                if not phB: return
                g.dma_start(out=u8[:],
                            in_=d_in["wdt"].rearrange("r e m -> r (e m)")
                            ).then_inc(sGu, 16)
                g.wait_ge(sS, S_bc)
                g.dma_start(out=Bnm.rearrange("p (n t) -> p n t", n=NST),
                            in_=bc_scr[0:NST].unsqueeze(0)
                            .broadcast_to([P, NST, TC])).then_inc(sGb, 16)
                g.dma_start(out=Cnm.rearrange("p (n t) -> p n t", n=NST),
                            in_=bc_scr[NST:32].unsqueeze(0)
                            .broadcast_to([P, NST, TC])).then_inc(sGb, 16)
                for et in range(NET):
                    if et >= 2:
                        g.wait_ge(sT, T_z(et - 2))
                    wsl = (wsA if et % 2 == 0 else wsB)[:, 0:NKD * P]
                    g.dma_start(out=wsl.rearrange("p (k m) -> p k m", k=NKD),
                                in_=d_in["winz"][et]).then_inc(sGs[et % 2], 16)
                n["sGs0"] += 16 * (NET // 2)
                n["sGs1"] += 16 * (NET // 2)
                n["sGu"] += 16
                n["sGb"] += 32

            @blk.sync
            def _(sync, ch=ch):
                if not phB: return
                for et in range(NET):
                    if et >= 2:
                        sync.wait_ge(sV, V_y(et - 2))
                    xt = xt0 if et % 2 == 0 else xt1
                    sync.dma_start(out=xt[:], in_=xt_scr[et]
                                   ).then_inc(sLd[et % 2], 16)
                n["sLd0"] += 16 * (NET // 2)
                n["sLd1"] += 16 * (NET // 2)

            @blk.tensor
            def _(t, ch=ch):
                if not phB: return
                t.wait_ge(sGu, G_wdt)
                for et in range(NET):
                    if et >= 1:
                        t.wait_ge(sA, A_zr(et - 1))
                    t.matmul(pb[4][:], u8[:, et * P:(et + 1) * P], dtlow[:],
                             start=True, stop=True).then_inc(sT, 1)
                    _sl, _v = G_zsl(et)
                    t.wait_ge(sGs[_sl], _v)
                    if et >= 2:
                        t.wait_ge(sV, V_y(et - 2))
                    wsl = (wsA if et % 2 == 0 else wsB)[:, 0:NKD * P]
                    pz = pb[et % 2]
                    for k in range(NKD):
                        mm = t.matmul(pz[:], wsl[:, k * P:(k + 1) * P],
                                      xbf[:, k * TC:(k + 1) * TC],
                                      start=(k == 0), stop=(k == NKD - 1))
                    mm.then_inc(sT, 1)
                n["sT"] += 2 * NET

            @blk.scalar
            def _(sc, ch=ch):
                if not phB: return
                for et in range(NET):
                    pz = pb[et % 2]
                    bcol = c_dtb[:, et:et + 1]
                    sc.wait_ge(sT, T_dt(et))
                    if et >= 1:
                        sc.wait_ge(sV, V_y(et - 1))
                    sc.activation(t1[:], pb[4][:], AF.Abs, bias=bcol)
                    sc.activation(t2[:], t1[:], AF.Exp, scale=-1.0)
                    sc.activation(t1[:], t2[:], AF.Ln, bias=1.0)
                    sc.activation(t2[:], pb[4][:], AF.Relu, bias=bcol).then_inc(sA, 1)
                    sc.wait_ge(sT, T_z(et))
                    sc.wait_ge(sV, V_dtf(et))
                    sc.activation(t1[:], pz[:], AF.Abs)
                    sc.activation(t2[:], t1[:], AF.Exp, scale=-1.0)
                    sc.activation(t1[:], t2[:], AF.Ln, bias=1.0)
                    sc.activation(t2[:], pz[:], AF.Relu, scale=-1.0).then_inc(sA, 1)
                    sc.wait_ge(sV, V_zsp(et))
                    sc.activation(t2[:], t1[:], AF.Exp, scale=-1.0).then_inc(sA, 1)
                    for nn in range(NST):
                        if nn >= 2:
                            sc.wait_ge(sV, V_sc(et, nn - 2))
                        dA = dA0 if nn % 2 == 0 else dA1
                        sc.activation(dA[:], dt_f[:], AF.Exp,
                                      scale=c_A[:, et * NST + nn:et * NST + nn + 1]
                                      ).then_inc(sA, 1)
                n["sA"] += 19 * NET

            @blk.vector
            def _(v, ch=ch):
                if not phB: return
                for et in range(NET):
                    pz = pb[et % 2]
                    if et == 0:
                        v.wait_ge(sGb, G_cnm)
                    v.wait_ge(sA, A_sp(et))
                    v.tensor_add(dt_f[:], t1[:], t2[:]).then_inc(sV, 1)
                    _sl, _v = S_xt(et)
                    v.wait_ge(sLd[_sl], _v)
                    xt = xt0 if et % 2 == 0 else xt1
                    v.tensor_mul(du[:], dt_f[:], xt[:])
                    v.wait_ge(sA, A_zr(et))
                    v.tensor_add(t1[:], t1[:], t2[:]).then_inc(sV, 1)
                    htn3 = htn.rearrange("p (t n) -> p t n", n=NST)
                    for nn in range(NST):
                        dA = dA0 if nn % 2 == 0 else dA1
                        wbuf = wb0 if nn % 2 == 0 else wb1
                        v.tensor_mul(wbuf[:], du[:],
                                     Bnm[:, nn * TC:(nn + 1) * TC])
                        v.wait_ge(sA, A_dA(et, nn))
                        v.tensor_tensor_scan(
                            htn3[:, :, nn], dA[:], wbuf[:],
                            hprev[:, et * NST + nn:et * NST + nn + 1],
                            ALU.mult, ALU.add).then_inc(sV, 1)
                    v.wait_ge(sV, V_sc(et, NST - 1))
                    v.tensor_copy(hprev[:, et * NST:(et + 1) * NST],
                                  htn[:, (TC - 1) * NST:TC * NST])
                    v.tensor_tensor(htn3[:, :, :], htn3[:, :, :],
                                    Cnm.rearrange("p (n t) -> p t n", n=NST),
                                    ALU.mult)
                    v.tensor_reduce(yred[:], htn3[:, :, :], AX.X, ALU.add)
                    v.scalar_tensor_tensor(ypre[:], xt[:], c_D[:, et:et + 1],
                                           yred[:], ALU.mult, ALU.add)
                    v.wait_ge(sA, A_sg(et))
                    v.tensor_mul(t1[:], ypre[:], t2[:])
                    v.tensor_tensor(ybf[:, et * TC:(et + 1) * TC], t1[:], pz[:],
                                    ALU.mult).then_inc(sV, 1)
                n["sV"] += 19 * NET

            if debug and phB:
                @blk.sync
                def _(sync, ch=ch):
                    sync.wait_ge(sV, n["sV"])
                    sync.wait_ge(sA, n["sA"])
                    sync.dma_start(out=dbg["dbg_dt"][ch], in_=dt_f[:]).then_inc(sS, 16)
                    sync.dma_start(out=dbg["dbg_du"][ch], in_=du[:]).then_inc(sS, 16)
                    sync.dma_start(out=dbg["dbg_yred"][ch], in_=yred[:]).then_inc(sS, 16)
                    xt = xt0 if (NET - 1) % 2 == 0 else xt1
                    sync.dma_start(out=dbg["dbg_xt"][ch], in_=xt[:]).then_inc(sS, 16)
                    sync.dma_start(out=dbg["dbg_proj"][ch], in_=bc32[:]).then_inc(sS, 16)
                    sync.dma_start(out=dbg["dbg_hprev"][ch], in_=hprev[:]).then_inc(sS, 16)
                    n["sS"] += 16 * 6

            barrier()

            phC = "C" in phases
            bG, bS, bT, bV, bA = n["sG"], n["sS"], n["sT"], n["sV"], n["sA"]
            bGx, bGu, bGb = n["sGx"], n["sGu"], n["sGb"]
            bSl, bSx = n["sSl"], n["sSx"]
            bGs = [n[f"sGs{i}"] for i in range(4)]
            bXr = [n[f"sXr{i}"] for i in range(2)]
            G_wo = lambda dt: (dt % 2, bGs[dt % 2] + 16 * (dt // 2 + 1))
            S_xr = lambda dt: (dt % 2, bXr[dt % 2] + 16 * (dt // 2 + 1))
            T_mm = lambda dt: bT + dt + 1
            T_st = bT + NDT + 2 * NDT
            V_stt = lambda dt: bV + dt + 1
            A_c = lambda dt: bA + 2 * (dt + 1)

            @blk.gpsimd
            def _(g, ch=ch):
                if not phC: return
                for dt in range(NDT):
                    if dt >= 2:
                        g.wait_ge(sT, T_mm(dt - 2))
                    wsl = (wsA if dt % 2 == 0 else wsB)[:, 0:NET * P]
                    g.dma_start(out=wsl.rearrange("p (k m) -> p k m", k=NET),
                                in_=d_in["wout"][dt]).then_inc(sGs[dt % 2], 16)
                n["sGs0"] += 16 * (NDT // 2)
                n["sGs1"] += 16 * (NDT // 2)

            @blk.sync
            def _(sync, ch=ch):
                if not phC: return
                for dt in range(NDT):
                    if dt >= 2:
                        sync.wait_ge(sV, V_stt(dt - 2))
                    xr = xt0 if dt % 2 == 0 else xt1
                    sync.dma_start(out=xr[:], in_=d_in["xt_host"][ch][dt]
                                   ).then_inc(sXr[dt % 2], 16)
                n["sXr0"] += 16 * (NDT // 2)
                n["sXr1"] += 16 * (NDT // 2)

            @blk.tensor
            def _(t, ch=ch):
                if not phC: return
                for dt in range(NDT):
                    _sl, _v = G_wo(dt)
                    t.wait_ge(sGs[_sl], _v)
                    if dt >= 2:
                        t.wait_ge(sV, V_stt(dt - 2))
                    wsl = (wsA if dt % 2 == 0 else wsB)[:, 0:NET * P]
                    po = pb[dt % 2]
                    for k in range(NET):
                        mm = t.matmul(po[:], wsl[:, k * P:(k + 1) * P],
                                      ybf[:, k * TC:(k + 1) * TC],
                                      start=(k == 0), stop=(k == NET - 1))
                    mm.then_inc(sT, 1)
                for dt in range(NDT):
                    t.wait_ge(sA, A_c(dt))
                    t.matmul(pb[2][0:1, :], ones_bf[:], xtb0[:],
                             start=(dt == 0), stop=(dt == NDT - 1)).then_inc(sT, 1)
                    t.matmul(pb[3][0:1, :], ones_bf[:], xtb1[:],
                             start=(dt == 0), stop=(dt == NDT - 1)).then_inc(sT, 1)
                t.wait_ge(sA, bA + 2 * NDT + 2)
                t.matmul(pb[4][:], ones_row[:], srow_b, start=True,
                         stop=True).then_inc(sT, 1)
                t.wait_ge(sV, bV + NDT + 5)
                t.matmul(pb[5][:], ones_row[:], trow, start=True,
                         stop=True).then_inc(sT, 1)
                n["sT"] += 3 * NDT + 2

            @blk.scalar
            def _(sc, ch=ch):
                if not phC: return
                for dt in range(NDT):
                    sc.wait_ge(sV, V_stt(dt))
                    if dt >= 1:
                        sc.wait_ge(sT, bT + NDT + 2 * dt)
                    sc.activation(xtb0[:], x_res[:, dt * TC:(dt + 1) * TC],
                                  AF.Copy).then_inc(sA, 1)
                    sc.activation(xtb1[:], x_res[:, dt * TC:(dt + 1) * TC],
                                  AF.Square).then_inc(sA, 1)
                n["sA"] += 2 * NDT
                sc.wait_ge(sV, bV + NDT + 4)
                sc.activation(trow, srow_b, AF.Ln, bias=c_row4[0:1, 0:1]).then_inc(sA, 1)
                sc.activation(srow_b, trow, AF.Exp, scale=c_row4[0:1, 1:2]).then_inc(sA, 1)
                n["sA"] += 2
                for dt in range(NDT):
                    sc.wait_ge(sV, bV + NDT + 5 + 2 * (dt + 1))
                    sc.activation(x1b[:, dt * TC:(dt + 1) * TC],
                                  x1[:, dt * TC:(dt + 1) * TC], AF.Copy
                                  ).then_inc(sA, 1)
                n["sA"] += NDT

            @blk.vector
            def _(v, ch=ch):
                if not phC: return
                for dt in range(NDT):
                    v.wait_ge(sT, T_mm(dt))
                    _sl, _v = S_xr(dt)
                    v.wait_ge(sXr[_sl], _v)
                    po = pb[dt % 2]
                    xr = xt0 if dt % 2 == 0 else xt1
                    v.scalar_tensor_tensor(x_res[:, dt * TC:(dt + 1) * TC],
                                           po[:], 1.0, xr[:], ALU.mult,
                                           ALU.add).then_inc(sV, 1)
                v.wait_ge(sT, T_st)
                v.tensor_scalar(srow_a, pb[2][0:1, :], c_row4[0:1, 2:3], None, ALU.mult).then_inc(sV, 1)
                v.tensor_scalar(srow_b, pb[3][0:1, :], c_row4[0:1, 2:3], None, ALU.mult).then_inc(sV, 1)
                v.tensor_mul(trow, srow_a, srow_a).then_inc(sV, 1)
                v.tensor_sub(srow_b, srow_b, trow).then_inc(sV, 1)
                v.wait_ge(sA, bA + 2 * NDT + 2)
                v.tensor_mul(trow, srow_a, srow_b).then_inc(sV, 1)
                n["sV"] += NDT + 5
                v.wait_ge(sT, bT + 3 * NDT + 2)
                for dt in range(NDT):
                    v.tensor_tensor(x1[:, dt * TC:(dt + 1) * TC],
                                    x_res[:, dt * TC:(dt + 1) * TC],
                                    pb[4][:], ALU.mult).then_inc(sV, 1)
                    v.tensor_sub(x1[:, dt * TC:(dt + 1) * TC],
                                 x1[:, dt * TC:(dt + 1) * TC],
                                 pb[5][:]).then_inc(sV, 1)
                n["sV"] += 2 * NDT

            if debug and phC:
                @blk.sync
                def _(sync, ch=ch):
                    sync.wait_ge(sV, n["sV"])
                    sync.dma_start(out=dbg["dbg_x1"][ch],
                                   in_=x1[:, 0:TC]).then_inc(sS, 16)
                    n["sS"] += 16

            barrier()

            phD = "D" in phases
            bG, bS, bT, bV, bA = n["sG"], n["sS"], n["sT"], n["sV"], n["sA"]
            bGs = [n[f"sGs{i}"] for i in range(4)]
            G_g = lambda it: (it % 2, bGs[it % 2] + 16 * (it // 2 + 1))
            G_u = lambda it: (2 + it % 2, bGs[2 + it % 2] + 16 * (it // 2 + 1))
            _ng = [NIT - NIT // 2, NIT // 2]
            def G_d(dt, q):
                idx = dt * 4 + q
                s = idx % 4
                prior = (NIT // 2 + NIT % 2) if s == 0 else (NIT // 2)
                if s >= 2:
                    prior = (NIT // 2 + NIT % 2) if s == 2 else (NIT // 2)
                return (s, bGs[s] + 16 * (prior + idx // 4 + 1))
            T_g = lambda it: bT + 2 * it + 1
            T_u = lambda it: bT + 2 * it + 2
            T_dnq = lambda dt, q: bT + 2 * NIT + 4 * dt + q + 1
            T_dn = lambda dt: bT + 2 * NIT + 4 * dt + 4
            T_st2 = bT + 2 * NIT + 4 * NDT + 2 * NDT
            T_bc2 = T_st2 + 2
            T_tr = lambda dt, j: T_bc2 + 4 * dt + j + 1
            A_sil = lambda it: bA + it + 1
            V_hdn = lambda it: bV + it + 1
            V_x2 = lambda dt: bV + NIT + dt + 1
            A_cst = lambda dt: bA + NIT + 2 * (dt + 1)
            V_nr = lambda dt: bV + NIT + NDT + 5 + dt + 1
            A_ev2 = lambda dt, j: bA + NIT + 2 * NDT + 2 + 4 * dt + j + 1

            gsl = [wsA[:, 0:2048], wsB[:, 0:2048]]
            usl = [wsA[:, 2048:4096], wsB[:, 2048:4096]]
            dsl = [wsA[:, 0:1408], wsA[:, 1408:2816],
                   wsB[:, 0:1408], wsB[:, 1408:2816]]

            @blk.gpsimd
            def _(g, ch=ch):
                if not phD: return
                for it in range(NIT):
                    if it >= 2:
                        g.wait_ge(sT, T_u(it - 2))
                    g.dma_start(out=gsl[it % 2].rearrange("p (k m) -> p k m", k=NKD),
                                in_=d_in["wg"][it]).then_inc(sGs[it % 2], 16)
                    g.dma_start(out=usl[it % 2].rearrange("p (k m) -> p k m", k=NKD),
                                in_=d_in["wu"][it]).then_inc(sGs[2 + it % 2], 16)
                for dt in range(NDT):
                    for q in range(4):
                        idx = dt * 4 + q
                        if idx >= 4:
                            g.wait_ge(sT, T_dnq((idx - 4) // 4, (idx - 4) % 4))
                        elif idx == 0:
                            g.wait_ge(sT, T_u(NIT - 1))
                        g.dma_start(out=dsl[idx % 4]
                                    .rearrange("p (k m) -> p k m", k=11),
                                    in_=d_in["wd"][dt, :, q * 11:(q + 1) * 11, :]
                                    ).then_inc(sGs[idx % 4], 16)
                n["sGs0"] += 16 * (NIT - NIT // 2 + NDT)
                n["sGs1"] += 16 * (NIT // 2 + NDT)
                n["sGs2"] += 16 * (NIT - NIT // 2 + NDT)
                n["sGs3"] += 16 * (NIT // 2 + NDT)

            @blk.tensor
            def _(t, ch=ch):
                if not phD: return
                for it in range(NIT):
                    _sl, _v = G_g(it)
                    t.wait_ge(sGs[_sl], _v)
                    if it >= 2:
                        t.wait_ge(sA, A_sil(it - 2))
                        t.wait_ge(sV, V_hdn(it - 2))
                    pg = pb[it % 2]
                    for k in range(NKD):
                        mm = t.matmul(pg[:], gsl[it % 2][:, k * P:(k + 1) * P],
                                      x1b[:, k * TC:(k + 1) * TC],
                                      start=(k == 0), stop=(k == NKD - 1))
                    mm.then_inc(sT, 1)
                    _sl, _v = G_u(it)
                    t.wait_ge(sGs[_sl], _v)
                    pu = pb[2 + it % 2]
                    for k in range(NKD):
                        mm = t.matmul(pu[:], usl[it % 2][:, k * P:(k + 1) * P],
                                      x1b[:, k * TC:(k + 1) * TC],
                                      start=(k == 0), stop=(k == NKD - 1))
                    mm.then_inc(sT, 1)
                for dt in range(NDT):
                    if dt == 0:
                        t.wait_ge(sV, V_hdn(NIT - 1))
                    if dt >= 2:
                        t.wait_ge(sV, V_x2(dt - 2))
                    pd = pb[dt % 2]
                    for q in range(4):
                        idx = dt * 4 + q
                        _sl, _v = G_d(dt, q)
                        t.wait_ge(sGs[_sl], _v)
                        for kk in range(11):
                            k = q * 11 + kk
                            mm = t.matmul(pd[:], dsl[idx % 4][:, kk * P:(kk + 1) * P],
                                          hdn[:, k * TC:(k + 1) * TC],
                                          start=(k == 0), stop=(k == NIT - 1))
                        mm.then_inc(sT, 1)
                for dt in range(NDT):
                    t.wait_ge(sA, A_cst(dt))
                    t.matmul(pb[2][0:1, :], ones_bf[:], xtb0[:],
                             start=(dt == 0), stop=(dt == NDT - 1)).then_inc(sT, 1)
                    t.matmul(pb[3][0:1, :], ones_bf[:], xtb1[:],
                             start=(dt == 0), stop=(dt == NDT - 1)).then_inc(sT, 1)
                t.wait_ge(sA, bA + NIT + 2 * NDT + 2)
                t.matmul(pb[4][:], ones_row[:], srow_b, start=True,
                         stop=True).then_inc(sT, 1)
                t.wait_ge(sV, bV + NIT + NDT + 5)
                t.matmul(pb[5][:], ones_row[:], trow, start=True,
                         stop=True).then_inc(sT, 1)
                for dt in range(NDT):
                    t.wait_ge(sV, V_nr(dt))
                    for j in range(4):
                        idx = dt * 4 + j
                        pt = pb[6 + idx % 2]
                        if idx >= 2:
                            t.wait_ge(sA, A_ev2((idx - 2) // 4, (idx - 2) % 4))
                        t.transpose(pt[:, 0:P], xt0[:, j * P:(j + 1) * P],
                                    c_id[:]).then_inc(sT, 1)
                n["sT"] += 2 * NIT + 4 * NDT + 2 * NDT + 2 + 4 * NDT

            @blk.scalar
            def _(sc, ch=ch):
                if not phD: return
                for it in range(NIT):
                    sc.wait_ge(sT, T_g(it))
                    if it >= 2:
                        sc.wait_ge(sV, V_hdn(it - 2))
                    sg = t1 if it % 2 == 0 else t2
                    sc.activation(sg[:], pb[it % 2][:], AF.Silu,
                                  scale=c_gs[:, it:it + 1]).then_inc(sA, 1)
                n["sA"] += NIT
                for dt in range(NDT):
                    sc.wait_ge(sV, V_x2(dt))
                    if dt >= 1:
                        sc.wait_ge(sT, bT + 2 * NIT + 4 * NDT + 2 * dt)
                    sc.activation(xtb0[:], x1[:, dt * TC:(dt + 1) * TC],
                                  AF.Copy).then_inc(sA, 1)
                    sc.activation(xtb1[:], x1[:, dt * TC:(dt + 1) * TC],
                                  AF.Square).then_inc(sA, 1)
                n["sA"] += 2 * NDT
                sc.wait_ge(sV, bV + NIT + NDT + 4)
                sc.activation(trow, srow_b, AF.Ln, bias=c_row4[0:1, 0:1]).then_inc(sA, 1)
                sc.activation(srow_b, trow, AF.Exp, scale=c_row4[0:1, 1:2]).then_inc(sA, 1)
                n["sA"] += 2
                for dt in range(NDT):
                    for j in range(4):
                        idx = dt * 4 + j
                        sc.wait_ge(sT, T_tr(dt, j))
                        pt = pb[6 + idx % 2]
                        sc.activation(obufs[j][:, dt * P:(dt + 1) * P],
                                      pt[:, 0:P], AF.Copy).then_inc(sA, 1)
                n["sA"] += 4 * NDT

            @blk.vector
            def _(v, ch=ch):
                if not phD: return
                for it in range(NIT):
                    v.wait_ge(sT, T_u(it))
                    v.wait_ge(sA, A_sil(it))
                    sg = t1 if it % 2 == 0 else t2
                    v.scalar_tensor_tensor(hdn[:, it * TC:(it + 1) * TC],
                                           pb[2 + it % 2][:], c_us[:, it:it + 1],
                                           sg[:], ALU.mult, ALU.mult
                                           ).then_inc(sV, 1)
                for dt in range(NDT):
                    v.wait_ge(sT, T_dn(dt))
                    v.scalar_tensor_tensor(x1[:, dt * TC:(dt + 1) * TC],
                                           pb[dt % 2][:], c_ds[:, dt:dt + 1],
                                           x1[:, dt * TC:(dt + 1) * TC],
                                           ALU.mult, ALU.add).then_inc(sV, 1)
                v.wait_ge(sT, T_st2)
                v.tensor_scalar(srow_a, pb[2][0:1, :], c_row4[0:1, 2:3], None, ALU.mult).then_inc(sV, 1)
                v.tensor_scalar(srow_b, pb[3][0:1, :], c_row4[0:1, 2:3], None, ALU.mult).then_inc(sV, 1)
                v.tensor_mul(trow, srow_a, srow_a).then_inc(sV, 1)
                v.tensor_sub(srow_b, srow_b, trow).then_inc(sV, 1)
                v.wait_ge(sA, bA + NIT + 2 * NDT + 2)
                v.tensor_mul(trow, srow_a, srow_b).then_inc(sV, 1)
                n["sV"] += NIT + NDT + 5
                v.wait_ge(sT, T_bc2)
                for dt in range(NDT):
                    if dt >= 1:
                        v.wait_ge(sT, T_tr(dt - 1, 3))
                    v.tensor_tensor(xt0[:], x1[:, dt * TC:(dt + 1) * TC],
                                    pb[4][:], ALU.mult)
                    v.tensor_sub(xt0[:], xt0[:], pb[5][:]).then_inc(sV, 1)
                n["sV"] += NDT

            @blk.sync
            def _(sync, ch=ch):
                if not phD: return
                sync.wait_ge(sA, bA + NIT + 2 * NDT + 2 + 4 * NDT)
                for j in range(4):
                    sync.dma_start(out=out_d[ch * TC + j * P: ch * TC + (j + 1) * P, :],
                                   in_=obufs[j][:]).then_inc(sS, 16)
                n["sS"] += 64

            if debug and phD:
                @blk.sync
                def _(sync, ch=ch):
                    sync.wait_ge(sV, n["sV"])
                    sync.dma_start(out=dbg["dbg_y"][ch],
                                   in_=x1[:, 0:TC]).then_inc(sS, 16)
                    n["sS"] += 16

            barrier()

        @blk.gpsimd
        def _(g):
            g.wait_ge(sS, n["sS"])

        @blk.sync
        def _(sync):
            sync.wait_ge(sS, n["sS"])

    return nc


_PROG_CACHE = {}


def _get_program(debug=False):
    key = bool(debug)
    if key not in _PROG_CACHE:
        _PROG_CACHE[key] = build_program(debug=debug)
    return _PROG_CACHE[key]


def kernel(**inputs):
    w = _host_prep(inputs)
    x = np.asarray(inputs["L0_attn_hidden_states"], np.float32)
    nc = _get_program(debug=False)
    in_maps = []
    for c in range(8):
        m = dict(w)
        m["xt_host"] = _host_x(x[c])
        in_maps.append(m)
    res = run_bass_kernel_spmd(nc, in_maps, core_ids=list(range(8)))
    return np.stack([r["out"] for r in res.results], axis=0)


# version 03
# speedup vs baseline: 1.0087x; 1.0087x over previous
import numpy as np
import concourse.bass as bass
import concourse.mybir as mybir
from concourse.bass_utils import run_bass_kernel_spmd

F32 = mybir.dt.float32
BF16 = mybir.dt.bfloat16
AF = mybir.ActivationFunctionType
ALU = mybir.AluOpType
AX = mybir.AxisListType

P = 128
L = 2048
DM = 2048
E = 4096
NST = 16
RK = 128
IT = 5632
TC = 512
NCH = L // TC
NET = E // P
NKD = DM // P
NDT = DM // P
NIT = IT // P
EPS = 1e-5
SKIP = set()
STRICT = set()


def _host_prep(inp):
    c = np.ascontiguousarray
    w = {}
    ipw = inp["L0_attn_in_proj_w"].astype(np.float32)
    w["winx"] = c(ipw[:E].reshape(NET, P, NKD, P).transpose(0, 3, 2, 1))
    w["winz"] = c(ipw[E:].reshape(NET, P, NKD, P).transpose(0, 3, 2, 1))
    xpw = inp["L0_attn_x_proj_w"].astype(np.float32)
    w["wxp1"] = c(xpw[:RK].T.reshape(NET, P, RK).transpose(1, 0, 2))
    w["wxp2"] = c(xpw[RK:].T.reshape(NET, P, 32).transpose(1, 0, 2))
    dtw = inp["L0_attn_dt_proj_w"].astype(np.float32)
    w["wdt"] = c(dtw.T.reshape(RK, NET, P))
    opw = inp["L0_attn_out_proj_w"].astype(np.float32)
    w["wout"] = c(opw.reshape(NDT, P, NET, P).transpose(0, 3, 2, 1))
    gw = inp["L0_ffn_gate_w"].astype(np.float32)
    w["wg"] = c(gw.reshape(NIT, P, NKD, P).transpose(0, 3, 2, 1))
    uw = inp["L0_ffn_up_w"].astype(np.float32)
    w["wu"] = c(uw.reshape(NIT, P, NKD, P).transpose(0, 3, 2, 1))
    dw = inp["L0_ffn_down_w"].astype(np.float32)
    w["wd"] = c(dw.reshape(NDT, P, NIT, P).transpose(0, 3, 2, 1))
    v32 = lambda v: c(v.astype(np.float32).reshape(-1, P).T)
    w["convw"] = c(inp["L0_attn_conv_w"].astype(np.float32)
                   .reshape(NET, P, 4).transpose(1, 0, 2).reshape(P, NET * 4))
    w["convb"] = v32(inp["L0_attn_conv_b"])
    w["dtb"] = v32(inp["L0_attn_dt_proj_b"])
    w["dvec"] = v32(inp["L0_attn_D"])
    w["alog"] = c(inp["L0_attn_A_log"].astype(np.float32)
                  .reshape(NET, P, NST).transpose(1, 0, 2).reshape(P, NET * NST))
    w["gs"] = v32(inp["L0_ffn_gate_s"])
    w["us"] = v32(inp["L0_ffn_up_s"])
    w["ds"] = v32(inp["L0_ffn_down_s"])
    w["ident"] = np.eye(P, dtype=np.float32)
    return w


def _host_x(x):
    return np.ascontiguousarray(
        x.astype(np.float32).reshape(NCH, TC, NKD, P).transpose(0, 2, 3, 1))


def build_program(debug=False, nch=NCH, phases="ABCD"):
    nc = bass.Bass("TRN2", target_bir_lowering=False)
    dt_ = nc.dram_tensor
    d_in = {}
    for name, shape in [
        ("winx", [NET, P, NKD, P]), ("winz", [NET, P, NKD, P]),
        ("wxp1", [P, NET, RK]), ("wxp2", [P, NET, 32]), ("wdt", [RK, NET, P]),
        ("wout", [NDT, P, NET, P]), ("wg", [NIT, P, NKD, P]),
        ("wu", [NIT, P, NKD, P]), ("wd", [NDT, P, NIT, P]),
        ("convw", [P, NET * 4]), ("convb", [P, NET]), ("dtb", [P, NET]),
        ("dvec", [P, NET]), ("alog", [P, NET * NST]),
        ("gs", [P, NIT]), ("us", [P, NIT]), ("ds", [P, NDT]),
        ("ident", [P, P]), ("xt_host", [NCH, NKD, P, TC]),
    ]:
        d_in[name] = dt_(name, shape, F32, kind="ExternalInput").ap()
    out_d = dt_("out", [L, DM], F32, kind="ExternalOutput").ap()
    xt_scr = dt_("xt_scr", [NET, P, TC], F32, kind="Internal").ap()
    bc_scr = dt_("bc_scr", [32, TC], F32, kind="Internal").ap()
    dbg = {}
    if debug:
        for name, shape in [("dbg_xt", [NCH, P, TC]), ("dbg_dt", [NCH, P, TC]),
                            ("dbg_yred", [NCH, P, TC]), ("dbg_du", [NCH, P, TC]),
                            ("dbg_x1", [NCH, P, TC]), ("dbg_proj", [NCH, 32, TC]),
                            ("dbg_y", [NCH, P, TC]), ("dbg_sg", [NCH, P, TC]),
                            ("dbg_hprev", [NCH, P, 512])]:
            dbg[name] = dt_(name, shape, F32, kind="ExternalOutput").ap()

    from contextlib import ExitStack
    es = ExitStack()
    sb = lambda n, s, d: es.enter_context(nc.sbuf_tensor(n, s, d))
    ps = lambda n, s: es.enter_context(nc.psum_tensor(n, s, F32))
    with es:
        c_convw = sb("c_convw", [P, NET * 4], F32)
        c_convb = sb("c_convb", [P, NET], F32)
        c_dtb = sb("c_dtb", [P, NET], F32)
        c_D = sb("c_D", [P, NET], F32)
        c_A = sb("c_A", [P, NET * NST], F32)
        c_gs = sb("c_gs", [P, NIT], F32)
        c_us = sb("c_us", [P, NIT], F32)
        c_ds = sb("c_ds", [P, NDT], F32)
        ones_bf = sb("ones_bf", [P, 1], BF16)
        ones_row = sb("ones_row", [1, P], F32)
        c_row4 = sb("c_row4", [1, 4], F32)
        c_id = sb("c_id", [P, P], F32)
        u8 = sb("u8", [P, NET * P], BF16)
        w_xp2 = sb("w_xp2", [P, NET * 32], BF16)
        hist = sb("hist", [P, NET * 3], F32)
        hprev = sb("hprev", [P, NET * NST], F32)
        cb0 = sb("cb0", [P, TC + 4], F32)
        cb1 = sb("cb1", [P, TC + 4], F32)
        xt0 = sb("xt0", [P, TC], F32)
        xt1 = sb("xt1", [P, TC], F32)
        xtb0 = sb("xtb0", [P, TC], BF16)
        xtb1 = sb("xtb1", [P, TC], BF16)
        dtlow = sb("dtlow", [P, TC], BF16)
        bc32 = sb("bc32", [32, TC], F32)
        dt_f = sb("dt_f", [P, TC], F32)
        du = sb("du", [P, TC], F32)
        t1 = sb("t1", [P, TC], F32)
        t2 = sb("t2", [P, TC], F32)
        dA0 = sb("dA0", [P, TC], F32)
        dA1 = sb("dA1", [P, TC], F32)
        wb0 = sb("wb0", [P, TC], F32)
        wb1 = sb("wb1", [P, TC], F32)
        yred = sb("yred", [P, TC], F32)
        wsA = sb("wsA", [P, 4096], BF16)
        wsB = sb("wsB", [P, 4096], BF16)
        xbf = sb("xbf", [P, NKD * TC], BF16)
        ar1 = sb("ar1", [P, 16384], F32)
        ar2 = sb("ar2", [P, 16384], F32)

        Bnm = ar1[:, 0:8192]
        Cnm = ar1[:, 8192:16384]
        x1 = ar1[:, 0:8192]
        x1b = ar1[:, 8192:12288].bitcast(BF16)
        ob0 = ar1[:, 12288:14336]
        ob1 = ar1[:, 14336:16384]
        htn = ar2[:, 0:8192]
        ybf = ar2[:, 8192:16384].bitcast(BF16)
        x_res = ar2[:, 0:8192]
        hdn = ar2[:, 0:11264].bitcast(BF16)
        ob2 = ar2[:, 11264:13312]
        ob3 = ar2[:, 13312:15360]
        obufs = [ob0, ob1, ob2, ob3]
        srow_a = du[0:1, :]
        srow_b = yred[0:1, :]
        trow = t1[0:1, :]

        pb = [ps(f"pb{i}", [P, TC]) for i in range(8)]

        sG = es.enter_context(nc.semaphore("sG"))
        sS = es.enter_context(nc.semaphore("sS"))
        sT = es.enter_context(nc.semaphore("sT"))
        sV = es.enter_context(nc.semaphore("sV"))
        sA = es.enter_context(nc.semaphore("sA"))
        sE = es.enter_context(nc.semaphore("sE"))
        sGx = es.enter_context(nc.semaphore("sGx"))
        sGu = es.enter_context(nc.semaphore("sGu"))
        sGb = es.enter_context(nc.semaphore("sGb"))
        sSl = es.enter_context(nc.semaphore("sSl"))
        sSx = es.enter_context(nc.semaphore("sSx"))
        sGs = [es.enter_context(nc.semaphore(f"sGs{i}")) for i in range(4)]
        sSt = [es.enter_context(nc.semaphore(f"sSt{i}")) for i in range(2)]
        sLd = [es.enter_context(nc.semaphore(f"sLd{i}")) for i in range(2)]
        sXr = [es.enter_context(nc.semaphore(f"sXr{i}")) for i in range(2)]
        sems = dict(sG=sG, sS=sS, sT=sT, sV=sV, sA=sA, sE=sE, sGx=sGx,
                    sGu=sGu, sGb=sGb, sSl=sSl, sSx=sSx)
        for _i in range(4):
            sems[f"sGs{_i}"] = sGs[_i]
        for _i in range(2):
            sems[f"sSt{_i}"] = sSt[_i]
            sems[f"sLd{_i}"] = sLd[_i]
            sems[f"sXr{_i}"] = sXr[_i]
        blk = es.enter_context(nc.Block())

        n = dict(sG=0, sS=0, sT=0, sV=0, sA=0, sE=0, sGx=0, sGu=0,
                 sGb=0, sSl=0, sSx=0,
                 sGs0=0, sGs1=0, sGs2=0, sGs3=0,
                 sSt0=0, sSt1=0, sLd0=0, sLd1=0, sXr0=0, sXr1=0)

        def barrier():
            snap = dict(n)
            allsems = tuple(sems.keys())
            for dec, eng_sems in [
                (blk.sync, allsems),
                (blk.gpsimd, allsems),
                (blk.tensor, allsems),
                (blk.vector, allsems),
                (blk.scalar, allsems),
            ]:
                def _b(eng, names=eng_sems):
                    for nm in names:
                        if snap[nm]:
                            eng.wait_ge(sems[nm], snap[nm])
                dec(_b)

        @blk.sync
        def _(sync):
            for dst, src in [(c_convw, d_in["convw"]), (c_convb, d_in["convb"]),
                             (c_dtb, d_in["dtb"]), (c_D, d_in["dvec"]),
                             (c_A, d_in["alog"]), (c_gs, d_in["gs"]),
                             (c_us, d_in["us"]), (c_ds, d_in["ds"]),
                             (c_id, d_in["ident"])]:
                sync.dma_start(out=dst[:], in_=src).then_inc(sS, 16)
                n["sS"] += 16

        @blk.scalar
        def _(sc):
            sc.wait_ge(sS, n["sS"])
            sc.activation(c_A[:], c_A[:], AF.Exp).then_inc(sA, 1)
            n["sA"] += 1

        @blk.vector
        def _(v):
            v.memset(ones_row[:], 1.0)
            v.memset(c_row4[0:1, 0:1], EPS)
            v.memset(c_row4[0:1, 1:2], -0.5)
            v.memset(c_row4[0:1, 2:3], 1.0 / DM)
            v.memset(ones_bf[:], 1.0)
            v.memset(hprev[:], 0.0)
            v.memset(hist[:], 0.0)
            v.wait_ge(sA, n["sA"])
            v.tensor_scalar(c_A[:], c_A[:], -1.0, None, ALU.mult).then_inc(sV, 1)
            n["sV"] += 1

        barrier()

        dbg_marks = {}

        for ch in range(nch):
            _sid = nc.enter_named_scope(f"A{ch}", False)[0]
            phA = "A" in phases
            Ag = phA or "g" in phases
            As = phA or "s" in phases
            At = phA or "t" in phases
            Aa = phA or "a" in phases
            Av = phA or "v" in phases
            bG, bS, bT, bV, bA = n["sG"], n["sS"], n["sT"], n["sV"], n["sA"]
            bE = n["sE"]
            bGx, bGu, bGb = n["sGx"], n["sGu"], n["sGb"]
            bSl, bSx = n["sSl"], n["sSx"]
            bGs = [n[f"sGs{i}"] for i in range(4)]
            bSt = [n[f"sSt{i}"] for i in range(2)]
            G_xbf = bGx + 16
            G_xp1 = bGu + 32
            G_slab = lambda et: (et % 2, bGs[et % 2] + 16 * (et // 2 + 1))
            S_st = lambda et: (et % 2, bSt[et % 2] + 16 * (et // 2 + 1))
            S_bc = bS + 16
            T_ip = lambda et: bT + 3 * et + 1
            T_xp = lambda et: bT + 3 * et + 3
            A_ev = lambda et: bA + 3 * et + 1
            A_si = lambda et: bA + 3 * et + 2
            A_ct = lambda et: bA + 3 * et + 3
            V_cv = lambda et: bV + et + 1

            @blk.gpsimd
            def _(g, ch=ch, bG=bG):
                if not Ag: return
                g.dma_start(out=xbf[:].rearrange("p (k t) -> p k t", k=NKD),
                            in_=d_in["xt_host"][ch].rearrange("k p t -> p k t")
                            ).then_inc(sGx, 16)
                g.dma_start(out=u8[:].rearrange("p (e m) -> p e m", e=NET),
                            in_=d_in["wxp1"]).then_inc(sGu, 16)
                g.dma_start(out=w_xp2[:].rearrange("p (e m) -> p e m", e=NET),
                            in_=d_in["wxp2"]).then_inc(sGu, 16)
                for et in range(NET):
                    if "g1" in STRICT and et >= 1:
                        g.wait_ge(sT, T_xp(et - 1))
                    if et >= 2:
                        g.wait_ge(sT, T_ip(et - 2))
                    wsl = (wsA if et % 2 == 0 else wsB)[:, 0:NKD * P]
                    g.dma_start(out=wsl.rearrange("p (k m) -> p k m", k=NKD),
                                in_=d_in["winx"][et]).then_inc(sGs[et % 2], 16)
                n["sGs0"] += 16 * (NET // 2)
                n["sGs1"] += 16 * (NET // 2)
                n["sGx"] += 16
                n["sGu"] += 32

            @blk.sync
            def _(sync, ch=ch):
                if not As: return
                for et in range(NET):
                    sync.wait_ge(sA, A_si(et))
                    xt = xt0 if et % 2 == 0 else xt1
                    sync.dma_start(out=xt_scr[et], in_=xt[:]
                                   ).then_inc(sSt[et % 2], 16)
                sync.wait_ge(sV, bV + NET + 1)
                sync.dma_start(out=bc_scr, in_=bc32[:]).then_inc(sS, 16)
                n["sSt0"] += 16 * (NET // 2)
                n["sSt1"] += 16 * (NET // 2)
                n["sS"] += 16

            @blk.tensor
            def _(t, ch=ch):
                if not At: return
                t.wait_ge(sGx, G_xbf)
                t.wait_ge(sGu, G_xp1)
                for et in range(NET):
                    sl, v_ = G_slab(et)
                    t.wait_ge(sGs[sl], v_)
                    if et >= 3:
                        t.wait_ge(sA, A_ev(et - 3))
                    wsl = (wsA if et % 2 == 0 else wsB)[:, 0:NKD * P]
                    px = pb[[0, 1, 4][et % 3]]
                    if "ip" not in SKIP:
                        for k in range(NKD):
                            mm = t.matmul(px[:], wsl[:, k * P:(k + 1) * P],
                                          xbf[:, k * TC:(k + 1) * TC],
                                          start=(k == 0), stop=(k == NKD - 1))
                        mm.then_inc(sT, 1)
                    else:
                        t.nop().then_inc(sT, 1)
                    t.wait_ge(sA, A_ct(et))
                    xb = xtb0 if et % 2 == 0 else xtb1
                    if "xp" not in SKIP:
                        t.matmul(pb[2][:], u8[:, et * P:(et + 1) * P], xb[:],
                                 start=(et == 0), stop=(et == NET - 1)).then_inc(sT, 1)
                        t.matmul(pb[3][0:32, :], w_xp2[:, et * 32:(et + 1) * 32], xb[:],
                                 start=(et == 0), stop=(et == NET - 1)).then_inc(sT, 1)
                    else:
                        t.nop().then_inc(sT, 1)
                        t.nop().then_inc(sT, 1)
                t.matmul(pb[6][:, 0:P], ones_row[:], ones_row[:],
                         start=True, stop=True).then_inc(sT, 1)
                n["sT"] += 3 * NET + 1

            @blk.scalar
            def _(sc, ch=ch):
                if not Aa: return
                for et in range(NET):
                    sc.wait_ge(sT, T_ip(et))
                    if et >= 2:
                        sc.wait_ge(sV, V_cv(et - 2))
                        sc.wait_ge(sE, bE + 2 * et - 2)
                        _sl, _v = S_st(et - 2)
                        sc.wait_ge(sSt[_sl], _v)
                    cb = cb0 if et % 2 == 0 else cb1
                    px = pb[[0, 1, 4][et % 3]]
                    if "ev" not in SKIP:
                        sc.activation(cb[:, 3:3 + TC], px[:], AF.Copy).then_inc(sA, 1)
                    else:
                        sc.nop().then_inc(sA, 1)
                    sc.wait_ge(sV, V_cv(et))
                    xt = xt0 if et % 2 == 0 else xt1
                    if "si" not in SKIP:
                        sc.activation(xt[:], t2[:], AF.Silu,
                                      bias=c_convb[:, et:et + 1]).then_inc(sA, 1)
                    else:
                        sc.nop().then_inc(sA, 1)
                    if et >= 1:
                        sc.wait_ge(sT, T_xp(et - 1))
                    xb = xtb0 if et % 2 == 0 else xtb1
                    if "ct" not in SKIP:
                        sc.activation(xb[:], xt[:], AF.Copy).then_inc(sA, 1)
                    else:
                        sc.nop().then_inc(sA, 1)
                n["sA"] += 3 * NET
                sc.wait_ge(sT, bT + 3 * NET + 1)
                sc.activation(dtlow[:], pb[2][:], AF.Copy).then_inc(sA, 1)
                n["sA"] += 1

            @blk.vector
            def _(v, ch=ch):
                if not Av: return
                for et in range(NET):
                    cb = cb0 if et % 2 == 0 else cb1
                    if "hc" not in SKIP:
                        v.tensor_copy(cb[:, 0:3], hist[:, et * 3:(et + 1) * 3]
                                      ).then_inc(sE, 1)
                        v.wait_ge(sE, bE + 2 * et + 1)
                    v.wait_ge(sA, A_ev(et))
                    if "cv" not in SKIP:
                        v.tensor_scalar(t1[:], cb[:, 0:TC],
                                        c_convw[:, et * 4:et * 4 + 1], None, ALU.mult)
                        v.scalar_tensor_tensor(t2[:], cb[:, 1:1 + TC],
                                               c_convw[:, et * 4 + 1:et * 4 + 2],
                                               t1[:], ALU.mult, ALU.add)
                        v.scalar_tensor_tensor(t1[:], cb[:, 2:2 + TC],
                                               c_convw[:, et * 4 + 2:et * 4 + 3],
                                               t2[:], ALU.mult, ALU.add)
                        v.scalar_tensor_tensor(t2[:], cb[:, 3:3 + TC],
                                               c_convw[:, et * 4 + 3:et * 4 + 4],
                                               t1[:], ALU.mult, ALU.add).then_inc(sV, 1)
                    else:
                        v.nop().then_inc(sV, 1)
                    if "hc" not in SKIP:
                        v.tensor_copy(hist[:, et * 3:(et + 1) * 3], cb[:, TC:TC + 3]
                                      ).then_inc(sE, 1)
                    n["sE"] += 2
                n["sV"] += NET
                v.wait_ge(sT, bT + 3 * NET + 1)
                v.tensor_copy(bc32[:], pb[3][0:32, :]).then_inc(sV, 1)
                n["sV"] += 1

            barrier()

            nc.leave_named_scope(f"A{ch}", _sid, False)
            _sid = nc.enter_named_scope(f"B{ch}", False)[0]
            phB = "B" in phases
            bG, bS, bT, bV, bA = n["sG"], n["sS"], n["sT"], n["sV"], n["sA"]
            bGx, bGu, bGb = n["sGx"], n["sGu"], n["sGb"]
            bSl, bSx = n["sSl"], n["sSx"]
            bGs = [n[f"sGs{i}"] for i in range(4)]
            bLd = [n[f"sLd{i}"] for i in range(2)]
            G_wdt = bGu + 16
            G_cnm = bGb + 32
            G_zsl = lambda et: (et % 2, bGs[et % 2] + 16 * (et // 2 + 1))
            S_xt = lambda et: (et % 2, bLd[et % 2] + 16 * (et // 2 + 1))
            T_dt = lambda et: bT + 2 * et + 1
            T_z = lambda et: bT + 2 * et + 2
            A_sp = lambda et: bA + 19 * et + 1
            A_zr = lambda et: bA + 19 * et + 2
            A_sg = lambda et: bA + 19 * et + 3
            A_dA = lambda et, nn: bA + 19 * et + 4 + nn
            V_dtf = lambda et: bV + 19 * et + 1
            V_zsp = lambda et: bV + 19 * et + 2
            V_sc = lambda et, nn: bV + 19 * et + 3 + nn
            V_y = lambda et: bV + 19 * et + 19

            @blk.gpsimd
            def _(g, ch=ch):
                if not phB: return
                g.dma_start(out=u8[:],
                            in_=d_in["wdt"].rearrange("r e m -> r (e m)")
                            ).then_inc(sGu, 16)
                g.wait_ge(sS, S_bc)
                g.dma_start(out=Bnm.rearrange("p (n t) -> p n t", n=NST),
                            in_=bc_scr[0:NST].unsqueeze(0)
                            .broadcast_to([P, NST, TC])).then_inc(sGb, 16)
                g.dma_start(out=Cnm.rearrange("p (n t) -> p n t", n=NST),
                            in_=bc_scr[NST:32].unsqueeze(0)
                            .broadcast_to([P, NST, TC])).then_inc(sGb, 16)
                for et in range(NET):
                    if et >= 2:
                        g.wait_ge(sT, T_z(et - 2))
                    wsl = (wsA if et % 2 == 0 else wsB)[:, 0:NKD * P]
                    g.dma_start(out=wsl.rearrange("p (k m) -> p k m", k=NKD),
                                in_=d_in["winz"][et]).then_inc(sGs[et % 2], 16)
                n["sGs0"] += 16 * (NET // 2)
                n["sGs1"] += 16 * (NET // 2)
                n["sGu"] += 16
                n["sGb"] += 32

            @blk.sync
            def _(sync, ch=ch):
                if not phB: return
                for et in range(NET):
                    if et >= 2:
                        sync.wait_ge(sV, V_y(et - 2))
                    xt = xt0 if et % 2 == 0 else xt1
                    sync.dma_start(out=xt[:], in_=xt_scr[et]
                                   ).then_inc(sLd[et % 2], 16)
                n["sLd0"] += 16 * (NET // 2)
                n["sLd1"] += 16 * (NET // 2)

            @blk.tensor
            def _(t, ch=ch):
                if not phB: return
                t.wait_ge(sGu, G_wdt)
                for et in range(NET):
                    if et >= 1:
                        t.wait_ge(sA, A_zr(et - 1))
                    t.matmul(pb[4][:], u8[:, et * P:(et + 1) * P], dtlow[:],
                             start=True, stop=True).then_inc(sT, 1)
                    _sl, _v = G_zsl(et)
                    t.wait_ge(sGs[_sl], _v)
                    if et >= 2:
                        t.wait_ge(sV, V_y(et - 2))
                    wsl = (wsA if et % 2 == 0 else wsB)[:, 0:NKD * P]
                    pz = pb[et % 2]
                    for k in range(NKD):
                        mm = t.matmul(pz[:], wsl[:, k * P:(k + 1) * P],
                                      xbf[:, k * TC:(k + 1) * TC],
                                      start=(k == 0), stop=(k == NKD - 1))
                    mm.then_inc(sT, 1)
                n["sT"] += 2 * NET

            @blk.scalar
            def _(sc, ch=ch):
                if not phB: return
                for et in range(NET):
                    pz = pb[et % 2]
                    bcol = c_dtb[:, et:et + 1]
                    sc.wait_ge(sT, T_dt(et))
                    if et >= 1:
                        sc.wait_ge(sV, V_y(et - 1))
                    sc.activation(t1[:], pb[4][:], AF.Abs, bias=bcol)
                    sc.activation(t2[:], t1[:], AF.Exp, scale=-1.0)
                    sc.activation(t1[:], t2[:], AF.Ln, bias=1.0)
                    sc.activation(t2[:], pb[4][:], AF.Relu, bias=bcol).then_inc(sA, 1)
                    sc.wait_ge(sT, T_z(et))
                    sc.wait_ge(sV, V_dtf(et))
                    sc.activation(t1[:], pz[:], AF.Abs)
                    sc.activation(t2[:], t1[:], AF.Exp, scale=-1.0)
                    sc.activation(t1[:], t2[:], AF.Ln, bias=1.0)
                    sc.activation(t2[:], pz[:], AF.Relu, scale=-1.0).then_inc(sA, 1)
                    sc.wait_ge(sV, V_zsp(et))
                    sc.activation(t2[:], t1[:], AF.Exp, scale=-1.0).then_inc(sA, 1)
                    for nn in range(NST):
                        if nn >= 2:
                            sc.wait_ge(sV, V_sc(et, nn - 2))
                        dA = dA0 if nn % 2 == 0 else dA1
                        sc.activation(dA[:], dt_f[:], AF.Exp,
                                      scale=c_A[:, et * NST + nn:et * NST + nn + 1]
                                      ).then_inc(sA, 1)
                n["sA"] += 19 * NET

            @blk.vector
            def _(v, ch=ch):
                if not phB: return
                for et in range(NET):
                    pz = pb[et % 2]
                    if et == 0:
                        v.wait_ge(sGb, G_cnm)
                    v.wait_ge(sA, A_sp(et))
                    v.tensor_add(dt_f[:], t1[:], t2[:]).then_inc(sV, 1)
                    _sl, _v = S_xt(et)
                    v.wait_ge(sLd[_sl], _v)
                    xt = xt0 if et % 2 == 0 else xt1
                    v.tensor_mul(du[:], dt_f[:], xt[:])
                    v.wait_ge(sA, A_zr(et))
                    v.tensor_add(t1[:], t1[:], t2[:]).then_inc(sV, 1)
                    htn3 = htn.rearrange("p (t n) -> p t n", n=NST)
                    for nn in range(NST):
                        dA = dA0 if nn % 2 == 0 else dA1
                        wbuf = wb0 if nn % 2 == 0 else wb1
                        v.tensor_mul(wbuf[:], du[:],
                                     Bnm[:, nn * TC:(nn + 1) * TC])
                        v.wait_ge(sA, A_dA(et, nn))
                        v.tensor_tensor_scan(
                            htn3[:, :, nn], dA[:], wbuf[:],
                            hprev[:, et * NST + nn:et * NST + nn + 1],
                            ALU.mult, ALU.add).then_inc(sV, 1)
                    v.wait_ge(sV, V_sc(et, NST - 1))
                    v.tensor_copy(hprev[:, et * NST:(et + 1) * NST],
                                  htn[:, (TC - 1) * NST:TC * NST])
                    v.tensor_tensor(htn3[:, :, :], htn3[:, :, :],
                                    Cnm.rearrange("p (n t) -> p t n", n=NST),
                                    ALU.mult)
                    v.tensor_reduce(yred[:], htn3[:, :, :], AX.X, ALU.add)
                    v.scalar_tensor_tensor(yred[:], xt[:], c_D[:, et:et + 1],
                                           yred[:], ALU.mult, ALU.add)
                    v.wait_ge(sA, A_sg(et))
                    v.tensor_mul(t1[:], yred[:], t2[:])
                    v.tensor_tensor(ybf[:, et * TC:(et + 1) * TC], t1[:], pz[:],
                                    ALU.mult).then_inc(sV, 1)
                n["sV"] += 19 * NET

            if debug and phB:
                @blk.sync
                def _(sync, ch=ch):
                    sync.wait_ge(sV, n["sV"])
                    sync.wait_ge(sA, n["sA"])
                    sync.dma_start(out=dbg["dbg_dt"][ch], in_=dt_f[:]).then_inc(sS, 16)
                    sync.dma_start(out=dbg["dbg_du"][ch], in_=du[:]).then_inc(sS, 16)
                    sync.dma_start(out=dbg["dbg_yred"][ch], in_=yred[:]).then_inc(sS, 16)
                    xt = xt0 if (NET - 1) % 2 == 0 else xt1
                    sync.dma_start(out=dbg["dbg_xt"][ch], in_=xt[:]).then_inc(sS, 16)
                    sync.dma_start(out=dbg["dbg_proj"][ch], in_=bc32[:]).then_inc(sS, 16)
                    sync.dma_start(out=dbg["dbg_hprev"][ch], in_=hprev[:]).then_inc(sS, 16)
                    n["sS"] += 16 * 6

            barrier()

            nc.leave_named_scope(f"B{ch}", _sid, False)
            _sid = nc.enter_named_scope(f"C{ch}", False)[0]
            phC = "C" in phases
            bG, bS, bT, bV, bA = n["sG"], n["sS"], n["sT"], n["sV"], n["sA"]
            bGx, bGu, bGb = n["sGx"], n["sGu"], n["sGb"]
            bSl, bSx = n["sSl"], n["sSx"]
            bGs = [n[f"sGs{i}"] for i in range(4)]
            bXr = [n[f"sXr{i}"] for i in range(2)]
            G_wo = lambda dt: (dt % 2, bGs[dt % 2] + 16 * (dt // 2 + 1))
            S_xr = lambda dt: (dt % 2, bXr[dt % 2] + 16 * (dt // 2 + 1))
            T_mm = lambda dt: bT + dt + 2 * max(0, dt - 2) + 1
            T_st = bT + 3 * NDT
            V_stt = lambda dt: bV + dt + 1
            A_c = lambda dt: bA + 2 * (dt + 1)

            @blk.gpsimd
            def _(g, ch=ch):
                if not phC: return
                for dt in range(NDT):
                    if dt >= 2:
                        g.wait_ge(sT, T_mm(dt - 2))
                    wsl = (wsA if dt % 2 == 0 else wsB)[:, 0:NET * P]
                    g.dma_start(out=wsl.rearrange("p (k m) -> p k m", k=NET),
                                in_=d_in["wout"][dt]).then_inc(sGs[dt % 2], 16)
                n["sGs0"] += 16 * (NDT // 2)
                n["sGs1"] += 16 * (NDT // 2)

            @blk.sync
            def _(sync, ch=ch):
                if not phC: return
                for dt in range(NDT):
                    if dt >= 2:
                        sync.wait_ge(sV, V_stt(dt - 2))
                    xr = xt0 if dt % 2 == 0 else xt1
                    sync.dma_start(out=xr[:], in_=d_in["xt_host"][ch][dt]
                                   ).then_inc(sXr[dt % 2], 16)
                n["sXr0"] += 16 * (NDT // 2)
                n["sXr1"] += 16 * (NDT // 2)

            @blk.tensor
            def _(t, ch=ch):
                if not phC: return
                for dt in range(NDT):
                    _sl, _v = G_wo(dt)
                    t.wait_ge(sGs[_sl], _v)
                    if dt >= 2:
                        t.wait_ge(sV, V_stt(dt - 2))
                    wsl = (wsA if dt % 2 == 0 else wsB)[:, 0:NET * P]
                    po = pb[dt % 2]
                    for k in range(NET):
                        mm = t.matmul(po[:], wsl[:, k * P:(k + 1) * P],
                                      ybf[:, k * TC:(k + 1) * TC],
                                      start=(k == 0), stop=(k == NET - 1))
                    mm.then_inc(sT, 1)
                    if dt >= 2:
                        t.wait_ge(sA, A_c(dt - 2))
                        t.matmul(pb[2][0:1, :], ones_bf[:], xtb0[:],
                                 start=(dt - 2 == 0), stop=(dt - 2 == NDT - 1)
                                 ).then_inc(sT, 1)
                        t.matmul(pb[3][0:1, :], ones_bf[:], xtb1[:],
                                 start=(dt - 2 == 0), stop=(dt - 2 == NDT - 1)
                                 ).then_inc(sT, 1)
                for dtl in range(NDT - 2, NDT):
                    t.wait_ge(sA, A_c(dtl))
                    t.matmul(pb[2][0:1, :], ones_bf[:], xtb0[:],
                             start=(dtl == 0), stop=(dtl == NDT - 1)).then_inc(sT, 1)
                    t.matmul(pb[3][0:1, :], ones_bf[:], xtb1[:],
                             start=(dtl == 0), stop=(dtl == NDT - 1)).then_inc(sT, 1)
                t.wait_ge(sA, bA + 2 * NDT + 2)
                t.matmul(pb[4][:], ones_row[:], srow_b, start=True,
                         stop=True).then_inc(sT, 1)
                t.wait_ge(sV, bV + NDT + 5)
                t.matmul(pb[5][:], ones_row[:], trow, start=True,
                         stop=True).then_inc(sT, 1)
                n["sT"] += 3 * NDT + 2

            @blk.scalar
            def _(sc, ch=ch):
                if not phC: return
                for dt in range(NDT):
                    sc.wait_ge(sV, V_stt(dt))
                    if dt >= 1:
                        j = dt - 1
                        pos = 3 * j + 5 if j <= NDT - 3 else 3 * NDT - 2 * (NDT - 1 - j)
                        sc.wait_ge(sT, bT + pos)
                    sc.activation(xtb0[:], x_res[:, dt * TC:(dt + 1) * TC],
                                  AF.Copy).then_inc(sA, 1)
                    sc.activation(xtb1[:], x_res[:, dt * TC:(dt + 1) * TC],
                                  AF.Square).then_inc(sA, 1)
                n["sA"] += 2 * NDT
                sc.wait_ge(sV, bV + NDT + 4)
                sc.activation(trow, srow_b, AF.Ln, bias=c_row4[0:1, 0:1]).then_inc(sA, 1)
                sc.activation(srow_b, trow, AF.Exp, scale=c_row4[0:1, 1:2]).then_inc(sA, 1)
                n["sA"] += 2
                for dt in range(NDT):
                    sc.wait_ge(sV, bV + NDT + 5 + 2 * (dt + 1))
                    sc.activation(x1b[:, dt * TC:(dt + 1) * TC],
                                  x1[:, dt * TC:(dt + 1) * TC], AF.Copy
                                  ).then_inc(sA, 1)
                n["sA"] += NDT

            @blk.vector
            def _(v, ch=ch):
                if not phC: return
                for dt in range(NDT):
                    v.wait_ge(sT, T_mm(dt))
                    _sl, _v = S_xr(dt)
                    v.wait_ge(sXr[_sl], _v)
                    po = pb[dt % 2]
                    xr = xt0 if dt % 2 == 0 else xt1
                    v.scalar_tensor_tensor(x_res[:, dt * TC:(dt + 1) * TC],
                                           po[:], 1.0, xr[:], ALU.mult,
                                           ALU.add).then_inc(sV, 1)
                v.wait_ge(sT, T_st)
                v.tensor_scalar(srow_a, pb[2][0:1, :], c_row4[0:1, 2:3], None, ALU.mult).then_inc(sV, 1)
                v.tensor_scalar(srow_b, pb[3][0:1, :], c_row4[0:1, 2:3], None, ALU.mult).then_inc(sV, 1)
                v.tensor_mul(trow, srow_a, srow_a).then_inc(sV, 1)
                v.tensor_sub(srow_b, srow_b, trow).then_inc(sV, 1)
                v.wait_ge(sA, bA + 2 * NDT + 2)
                v.tensor_mul(trow, srow_a, srow_b).then_inc(sV, 1)
                n["sV"] += NDT + 5
                v.wait_ge(sT, bT + 3 * NDT + 2)
                for dt in range(NDT):
                    v.tensor_tensor(x1[:, dt * TC:(dt + 1) * TC],
                                    x_res[:, dt * TC:(dt + 1) * TC],
                                    pb[4][:], ALU.mult).then_inc(sV, 1)
                    v.tensor_sub(x1[:, dt * TC:(dt + 1) * TC],
                                 x1[:, dt * TC:(dt + 1) * TC],
                                 pb[5][:]).then_inc(sV, 1)
                n["sV"] += 2 * NDT

            if debug and phC:
                @blk.sync
                def _(sync, ch=ch):
                    sync.wait_ge(sV, n["sV"])
                    sync.dma_start(out=dbg["dbg_x1"][ch],
                                   in_=x1[:, 0:TC]).then_inc(sS, 16)
                    n["sS"] += 16

            barrier()

            nc.leave_named_scope(f"C{ch}", _sid, False)
            _sid = nc.enter_named_scope(f"D{ch}", False)[0]
            phD = "D" in phases
            bG, bS, bT, bV, bA = n["sG"], n["sS"], n["sT"], n["sV"], n["sA"]
            bGs = [n[f"sGs{i}"] for i in range(4)]
            G_g = lambda it: (it % 2, bGs[it % 2] + 16 * (it // 2 + 1))
            G_u = lambda it: (2 + it % 2, bGs[2 + it % 2] + 16 * (it // 2 + 1))
            _ng = [NIT - NIT // 2, NIT // 2]
            def G_d(dt, q):
                idx = dt * 4 + q
                s = idx % 4
                prior = (NIT // 2 + NIT % 2) if s == 0 else (NIT // 2)
                if s >= 2:
                    prior = (NIT // 2 + NIT % 2) if s == 2 else (NIT // 2)
                return (s, bGs[s] + 16 * (prior + idx // 4 + 1))
            T_g = lambda it: bT + 2 * it + 1
            T_u = lambda it: bT + 2 * it + 2
            T_dnq = lambda dt, q: bT + 2 * NIT + 4 * dt + q + 1
            T_dn = lambda dt: bT + 2 * NIT + 4 * dt + 4
            T_st2 = bT + 2 * NIT + 4 * NDT + 2 * NDT
            T_bc2 = T_st2 + 2
            T_tr = lambda dt, j: T_bc2 + 4 * dt + j + 1
            A_sil = lambda it: bA + it + 1
            V_hdn = lambda it: bV + it + 1
            V_x2 = lambda dt: bV + NIT + dt + 1
            A_cst = lambda dt: bA + NIT + 2 * (dt + 1)
            V_nr = lambda dt: bV + NIT + NDT + 5 + dt + 1
            A_ev2 = lambda dt, j: bA + NIT + 2 * NDT + 2 + 4 * dt + j + 1

            gsl = [wsA[:, 0:2048], wsB[:, 0:2048]]
            usl = [wsA[:, 2048:4096], wsB[:, 2048:4096]]
            dsl = [wsA[:, 0:1408], wsA[:, 1408:2816],
                   wsB[:, 0:1408], wsB[:, 1408:2816]]

            @blk.gpsimd
            def _(g, ch=ch):
                if not phD: return
                for it in range(NIT):
                    if it >= 2:
                        g.wait_ge(sT, T_u(it - 2))
                    g.dma_start(out=gsl[it % 2].rearrange("p (k m) -> p k m", k=NKD),
                                in_=d_in["wg"][it]).then_inc(sGs[it % 2], 16)
                    g.dma_start(out=usl[it % 2].rearrange("p (k m) -> p k m", k=NKD),
                                in_=d_in["wu"][it]).then_inc(sGs[2 + it % 2], 16)
                for dt in range(NDT):
                    for q in range(4):
                        idx = dt * 4 + q
                        if idx >= 4:
                            g.wait_ge(sT, T_dnq((idx - 4) // 4, (idx - 4) % 4))
                        elif idx == 0:
                            g.wait_ge(sT, T_u(NIT - 1))
                        g.dma_start(out=dsl[idx % 4]
                                    .rearrange("p (k m) -> p k m", k=11),
                                    in_=d_in["wd"][dt, :, q * 11:(q + 1) * 11, :]
                                    ).then_inc(sGs[idx % 4], 16)
                n["sGs0"] += 16 * (NIT - NIT // 2 + NDT)
                n["sGs1"] += 16 * (NIT // 2 + NDT)
                n["sGs2"] += 16 * (NIT - NIT // 2 + NDT)
                n["sGs3"] += 16 * (NIT // 2 + NDT)

            @blk.tensor
            def _(t, ch=ch):
                if not phD: return
                for it in range(NIT):
                    _sl, _v = G_g(it)
                    t.wait_ge(sGs[_sl], _v)
                    if it >= 2:
                        t.wait_ge(sA, A_sil(it - 2))
                        t.wait_ge(sV, V_hdn(it - 2))
                    pg = pb[it % 2]
                    for k in range(NKD):
                        mm = t.matmul(pg[:], gsl[it % 2][:, k * P:(k + 1) * P],
                                      x1b[:, k * TC:(k + 1) * TC],
                                      start=(k == 0), stop=(k == NKD - 1))
                    mm.then_inc(sT, 1)
                    _sl, _v = G_u(it)
                    t.wait_ge(sGs[_sl], _v)
                    pu = pb[2 + it % 2]
                    for k in range(NKD):
                        mm = t.matmul(pu[:], usl[it % 2][:, k * P:(k + 1) * P],
                                      x1b[:, k * TC:(k + 1) * TC],
                                      start=(k == 0), stop=(k == NKD - 1))
                    mm.then_inc(sT, 1)
                for dt in range(NDT):
                    if dt == 0:
                        t.wait_ge(sV, V_hdn(NIT - 1))
                    if dt >= 2:
                        t.wait_ge(sV, V_x2(dt - 2))
                    pd = pb[dt % 2]
                    for q in range(4):
                        idx = dt * 4 + q
                        _sl, _v = G_d(dt, q)
                        t.wait_ge(sGs[_sl], _v)
                        for kk in range(11):
                            k = q * 11 + kk
                            mm = t.matmul(pd[:], dsl[idx % 4][:, kk * P:(kk + 1) * P],
                                          hdn[:, k * TC:(k + 1) * TC],
                                          start=(k == 0), stop=(k == NIT - 1))
                        mm.then_inc(sT, 1)
                for dt in range(NDT):
                    t.wait_ge(sA, A_cst(dt))
                    t.matmul(pb[2][0:1, :], ones_bf[:], xtb0[:],
                             start=(dt == 0), stop=(dt == NDT - 1)).then_inc(sT, 1)
                    t.matmul(pb[3][0:1, :], ones_bf[:], xtb1[:],
                             start=(dt == 0), stop=(dt == NDT - 1)).then_inc(sT, 1)
                t.wait_ge(sA, bA + NIT + 2 * NDT + 2)
                t.matmul(pb[4][:], ones_row[:], srow_b, start=True,
                         stop=True).then_inc(sT, 1)
                t.wait_ge(sV, bV + NIT + NDT + 5)
                t.matmul(pb[5][:], ones_row[:], trow, start=True,
                         stop=True).then_inc(sT, 1)
                for dt in range(NDT):
                    t.wait_ge(sV, V_nr(dt))
                    for j in range(4):
                        idx = dt * 4 + j
                        pt = pb[6 + idx % 2]
                        if idx >= 2:
                            t.wait_ge(sA, A_ev2((idx - 2) // 4, (idx - 2) % 4))
                        t.transpose(pt[:, 0:P], xt0[:, j * P:(j + 1) * P],
                                    c_id[:]).then_inc(sT, 1)
                n["sT"] += 2 * NIT + 4 * NDT + 2 * NDT + 2 + 4 * NDT

            @blk.scalar
            def _(sc, ch=ch):
                if not phD: return
                for it in range(NIT):
                    sc.wait_ge(sT, T_g(it))
                    if it >= 2:
                        sc.wait_ge(sV, V_hdn(it - 2))
                    sg = t1 if it % 2 == 0 else t2
                    sc.activation(sg[:], pb[it % 2][:], AF.Silu,
                                  scale=c_gs[:, it:it + 1]).then_inc(sA, 1)
                n["sA"] += NIT
                for dt in range(NDT):
                    sc.wait_ge(sV, V_x2(dt))
                    if dt >= 1:
                        sc.wait_ge(sT, bT + 2 * NIT + 4 * NDT + 2 * dt)
                    sc.activation(xtb0[:], x1[:, dt * TC:(dt + 1) * TC],
                                  AF.Copy).then_inc(sA, 1)
                    sc.activation(xtb1[:], x1[:, dt * TC:(dt + 1) * TC],
                                  AF.Square).then_inc(sA, 1)
                n["sA"] += 2 * NDT
                sc.wait_ge(sV, bV + NIT + NDT + 4)
                sc.activation(trow, srow_b, AF.Ln, bias=c_row4[0:1, 0:1]).then_inc(sA, 1)
                sc.activation(srow_b, trow, AF.Exp, scale=c_row4[0:1, 1:2]).then_inc(sA, 1)
                n["sA"] += 2
                for dt in range(NDT):
                    for j in range(4):
                        idx = dt * 4 + j
                        sc.wait_ge(sT, T_tr(dt, j))
                        pt = pb[6 + idx % 2]
                        sc.activation(obufs[j][:, dt * P:(dt + 1) * P],
                                      pt[:, 0:P], AF.Copy).then_inc(sA, 1)
                n["sA"] += 4 * NDT

            @blk.vector
            def _(v, ch=ch):
                if not phD: return
                for it in range(NIT):
                    v.wait_ge(sT, T_u(it))
                    v.wait_ge(sA, A_sil(it))
                    sg = t1 if it % 2 == 0 else t2
                    v.scalar_tensor_tensor(hdn[:, it * TC:(it + 1) * TC],
                                           pb[2 + it % 2][:], c_us[:, it:it + 1],
                                           sg[:], ALU.mult, ALU.mult
                                           ).then_inc(sV, 1)
                for dt in range(NDT):
                    v.wait_ge(sT, T_dn(dt))
                    v.scalar_tensor_tensor(x1[:, dt * TC:(dt + 1) * TC],
                                           pb[dt % 2][:], c_ds[:, dt:dt + 1],
                                           x1[:, dt * TC:(dt + 1) * TC],
                                           ALU.mult, ALU.add).then_inc(sV, 1)
                v.wait_ge(sT, T_st2)
                v.tensor_scalar(srow_a, pb[2][0:1, :], c_row4[0:1, 2:3], None, ALU.mult).then_inc(sV, 1)
                v.tensor_scalar(srow_b, pb[3][0:1, :], c_row4[0:1, 2:3], None, ALU.mult).then_inc(sV, 1)
                v.tensor_mul(trow, srow_a, srow_a).then_inc(sV, 1)
                v.tensor_sub(srow_b, srow_b, trow).then_inc(sV, 1)
                v.wait_ge(sA, bA + NIT + 2 * NDT + 2)
                v.tensor_mul(trow, srow_a, srow_b).then_inc(sV, 1)
                n["sV"] += NIT + NDT + 5
                v.wait_ge(sT, T_bc2)
                for dt in range(NDT):
                    if dt >= 1:
                        v.wait_ge(sT, T_tr(dt - 1, 3))
                    v.tensor_tensor(xt0[:], x1[:, dt * TC:(dt + 1) * TC],
                                    pb[4][:], ALU.mult)
                    v.tensor_sub(xt0[:], xt0[:], pb[5][:]).then_inc(sV, 1)
                n["sV"] += NDT

            @blk.sync
            def _(sync, ch=ch):
                if not phD: return
                sync.wait_ge(sA, bA + NIT + 2 * NDT + 2 + 4 * NDT)
                for j in range(4):
                    sync.dma_start(out=out_d[ch * TC + j * P: ch * TC + (j + 1) * P, :],
                                   in_=obufs[j][:]).then_inc(sS, 16)
                n["sS"] += 64

            if debug and phD:
                @blk.sync
                def _(sync, ch=ch):
                    sync.wait_ge(sV, n["sV"])
                    sync.dma_start(out=dbg["dbg_y"][ch],
                                   in_=x1[:, 0:TC]).then_inc(sS, 16)
                    n["sS"] += 16

            barrier()

            nc.leave_named_scope(f"D{ch}", _sid, False)
        @blk.gpsimd
        def _(g):
            g.wait_ge(sS, n["sS"])

        @blk.sync
        def _(sync):
            sync.wait_ge(sS, n["sS"])

    return nc


_PROG_CACHE = {}


def _get_program(debug=False):
    key = bool(debug)
    if key not in _PROG_CACHE:
        _PROG_CACHE[key] = build_program(debug=debug)
    return _PROG_CACHE[key]


def kernel(**inputs):
    w = _host_prep(inputs)
    x = np.asarray(inputs["L0_attn_hidden_states"], np.float32)
    nc = _get_program(debug=False)
    in_maps = []
    for c in range(8):
        m = dict(w)
        m["xt_host"] = _host_x(x[c])
        in_maps.append(m)
    res = run_bass_kernel_spmd(nc, in_maps, core_ids=list(range(8)))
    return np.stack([r["out"] for r in res.results], axis=0)


# version 04
# speedup vs baseline: 1.0180x; 1.0093x over previous
import numpy as np
import concourse.bass as bass
import concourse.mybir as mybir
from concourse.bass_utils import run_bass_kernel_spmd

F32 = mybir.dt.float32
BF16 = mybir.dt.bfloat16
AF = mybir.ActivationFunctionType
ALU = mybir.AluOpType
AX = mybir.AxisListType

P = 128
L = 2048
DM = 2048
E = 4096
NST = 16
RK = 128
IT = 5632
TC = 512
NCH = L // TC
NET = E // P
NKD = DM // P
NDT = DM // P
NIT = IT // P
EPS = 1e-5
SKIP = set()
STRICT = set()


def _host_prep(inp):
    c = np.ascontiguousarray
    w = {}
    ipw = inp["L0_attn_in_proj_w"].astype(np.float32)
    w["winx"] = c(ipw[:E].reshape(NET, P, NKD, P).transpose(0, 3, 2, 1))
    w["winz"] = c(ipw[E:].reshape(NET, P, NKD, P).transpose(0, 3, 2, 1))
    xpw = inp["L0_attn_x_proj_w"].astype(np.float32)
    w["wxp1"] = c(xpw[:RK].T.reshape(NET, P, RK).transpose(1, 0, 2))
    w["wxp2"] = c(xpw[RK:].T.reshape(NET, P, 32).transpose(1, 0, 2))
    dtw = inp["L0_attn_dt_proj_w"].astype(np.float32)
    w["wdt"] = c(dtw.T.reshape(RK, NET, P))
    opw = inp["L0_attn_out_proj_w"].astype(np.float32)
    w["wout"] = c(opw.reshape(NDT, P, NET, P).transpose(0, 3, 2, 1))
    gw = inp["L0_ffn_gate_w"].astype(np.float32)
    w["wg"] = c(gw.reshape(NIT, P, NKD, P).transpose(0, 3, 2, 1))
    uw = inp["L0_ffn_up_w"].astype(np.float32)
    w["wu"] = c(uw.reshape(NIT, P, NKD, P).transpose(0, 3, 2, 1))
    dw = inp["L0_ffn_down_w"].astype(np.float32)
    w["wd"] = c(dw.reshape(NDT, P, NIT, P).transpose(0, 3, 2, 1))
    v32 = lambda v: c(v.astype(np.float32).reshape(-1, P).T)
    w["convw"] = c(inp["L0_attn_conv_w"].astype(np.float32)
                   .reshape(NET, P, 4).transpose(1, 0, 2).reshape(P, NET * 4))
    w["convb"] = v32(inp["L0_attn_conv_b"])
    w["dtb"] = v32(inp["L0_attn_dt_proj_b"])
    w["dvec"] = v32(inp["L0_attn_D"])
    w["alog"] = c(inp["L0_attn_A_log"].astype(np.float32)
                  .reshape(NET, P, NST).transpose(1, 0, 2).reshape(P, NET * NST))
    w["gs"] = v32(inp["L0_ffn_gate_s"])
    w["us"] = v32(inp["L0_ffn_up_s"])
    w["ds"] = v32(inp["L0_ffn_down_s"])
    w["ident"] = np.eye(P, dtype=np.float32)
    return w


def _host_x(x):
    return np.ascontiguousarray(
        x.astype(np.float32).reshape(NCH, TC, NKD, P).transpose(0, 2, 3, 1))


def build_program(debug=False, nch=NCH, phases="ABCD"):
    nc = bass.Bass("TRN2", target_bir_lowering=False)
    dt_ = nc.dram_tensor
    d_in = {}
    for name, shape in [
        ("winx", [NET, P, NKD, P]), ("winz", [NET, P, NKD, P]),
        ("wxp1", [P, NET, RK]), ("wxp2", [P, NET, 32]), ("wdt", [RK, NET, P]),
        ("wout", [NDT, P, NET, P]), ("wg", [NIT, P, NKD, P]),
        ("wu", [NIT, P, NKD, P]), ("wd", [NDT, P, NIT, P]),
        ("convw", [P, NET * 4]), ("convb", [P, NET]), ("dtb", [P, NET]),
        ("dvec", [P, NET]), ("alog", [P, NET * NST]),
        ("gs", [P, NIT]), ("us", [P, NIT]), ("ds", [P, NDT]),
        ("ident", [P, P]), ("xt_host", [NCH, NKD, P, TC]),
    ]:
        d_in[name] = dt_(name, shape, F32, kind="ExternalInput").ap()
    out_d = dt_("out", [L, DM], F32, kind="ExternalOutput").ap()
    xt_scr = dt_("xt_scr", [NET, P, TC], F32, kind="Internal").ap()
    bc_scr = dt_("bc_scr", [32, TC], F32, kind="Internal").ap()
    dbg = {}
    if debug:
        for name, shape in [("dbg_xt", [NCH, P, TC]), ("dbg_dt", [NCH, P, TC]),
                            ("dbg_yred", [NCH, P, TC]), ("dbg_du", [NCH, P, TC]),
                            ("dbg_x1", [NCH, P, TC]), ("dbg_proj", [NCH, 32, TC]),
                            ("dbg_y", [NCH, P, TC]), ("dbg_sg", [NCH, P, TC]),
                            ("dbg_hprev", [NCH, P, 512])]:
            dbg[name] = dt_(name, shape, F32, kind="ExternalOutput").ap()

    from contextlib import ExitStack
    es = ExitStack()
    sb = lambda n, s, d: es.enter_context(nc.sbuf_tensor(n, s, d))
    ps = lambda n, s: es.enter_context(nc.psum_tensor(n, s, F32))
    with es:
        c_convw = sb("c_convw", [P, NET * 4], F32)
        c_convb = sb("c_convb", [P, NET], F32)
        c_dtb = sb("c_dtb", [P, NET], F32)
        c_D = sb("c_D", [P, NET], F32)
        c_A = sb("c_A", [P, NET * NST], F32)
        c_gs = sb("c_gs", [P, NIT], F32)
        c_us = sb("c_us", [P, NIT], F32)
        c_ds = sb("c_ds", [P, NDT], F32)
        ones_bf = sb("ones_bf", [P, 1], BF16)
        ones_row = sb("ones_row", [1, P], F32)
        c_row4 = sb("c_row4", [1, 4], F32)
        c_id = sb("c_id", [P, P], F32)
        u8 = sb("u8", [P, NET * P], BF16)
        w_xp2 = sb("w_xp2", [P, NET * 32], BF16)
        hist = sb("hist", [P, NET * 3], F32)
        hprev = sb("hprev", [P, NET * NST], F32)
        cb0 = sb("cb0", [P, TC + 4], F32)
        cb1 = sb("cb1", [P, TC + 4], F32)
        xt0 = sb("xt0", [P, TC], F32)
        xt1 = sb("xt1", [P, TC], F32)
        xtb2 = sb("xtb2", [P, TC], F32)
        _xtbv = xtb2[:].bitcast(BF16)
        xtb0 = _xtbv[:, 0:TC]
        xtb1 = _xtbv[:, TC:2 * TC]
        dtlow = sb("dtlow", [P, TC], BF16)
        bc32 = sb("bc32", [32, TC], F32)
        dt_f = sb("dt_f", [P, TC], F32)
        dt_f1 = xtb2
        du = sb("du", [P, TC], F32)
        t1 = sb("t1", [P, TC], F32)
        t2 = sb("t2", [P, TC], F32)
        dA0 = sb("dA0", [P, TC], F32)
        dA1 = sb("dA1", [P, TC], F32)
        wb0 = sb("wb0", [P, TC], F32)
        wb1 = sb("wb1", [P, TC], F32)
        yred = sb("yred", [P, TC], F32)
        wsA = sb("wsA", [P, 4096], BF16)
        wsB = sb("wsB", [P, 4096], BF16)
        xbf = sb("xbf", [P, NKD * TC], BF16)
        ar1 = sb("ar1", [P, 16384], F32)
        ar2 = sb("ar2", [P, 16384], F32)

        Bnm = ar1[:, 0:8192]
        Cnm = ar1[:, 8192:16384]
        x1 = ar1[:, 0:8192]
        x1b = ar1[:, 8192:12288].bitcast(BF16)
        ob0 = ar1[:, 12288:14336]
        ob1 = ar1[:, 14336:16384]
        htn = ar2[:, 0:8192]
        ybf = ar2[:, 8192:16384].bitcast(BF16)
        x_res = ar2[:, 0:8192]
        hdn = ar2[:, 0:11264].bitcast(BF16)
        ob2 = ar2[:, 11264:13312]
        ob3 = ar2[:, 13312:15360]
        obufs = [ob0, ob1, ob2, ob3]
        du1 = cb0[:, 0:TC]
        sg = cb1[:, 0:TC]
        srow_a = du[0:1, :]
        srow_b = yred[0:1, :]
        trow = t1[0:1, :]

        pb = [ps(f"pb{i}", [P, TC]) for i in range(8)]

        sG = es.enter_context(nc.semaphore("sG"))
        sS = es.enter_context(nc.semaphore("sS"))
        sT = es.enter_context(nc.semaphore("sT"))
        sV = es.enter_context(nc.semaphore("sV"))
        sA = es.enter_context(nc.semaphore("sA"))
        sE = es.enter_context(nc.semaphore("sE"))
        sGx = es.enter_context(nc.semaphore("sGx"))
        sGu = es.enter_context(nc.semaphore("sGu"))
        sGb = es.enter_context(nc.semaphore("sGb"))
        sSl = es.enter_context(nc.semaphore("sSl"))
        sSx = es.enter_context(nc.semaphore("sSx"))
        sGs = [es.enter_context(nc.semaphore(f"sGs{i}")) for i in range(4)]
        sSt = [es.enter_context(nc.semaphore(f"sSt{i}")) for i in range(2)]
        sLd = [es.enter_context(nc.semaphore(f"sLd{i}")) for i in range(2)]
        sXr = [es.enter_context(nc.semaphore(f"sXr{i}")) for i in range(2)]
        sems = dict(sG=sG, sS=sS, sT=sT, sV=sV, sA=sA, sE=sE, sGx=sGx,
                    sGu=sGu, sGb=sGb, sSl=sSl, sSx=sSx)
        for _i in range(4):
            sems[f"sGs{_i}"] = sGs[_i]
        for _i in range(2):
            sems[f"sSt{_i}"] = sSt[_i]
            sems[f"sLd{_i}"] = sLd[_i]
            sems[f"sXr{_i}"] = sXr[_i]
        blk = es.enter_context(nc.Block())

        n = dict(sG=0, sS=0, sT=0, sV=0, sA=0, sE=0, sGx=0, sGu=0,
                 sGb=0, sSl=0, sSx=0,
                 sGs0=0, sGs1=0, sGs2=0, sGs3=0,
                 sSt0=0, sSt1=0, sLd0=0, sLd1=0, sXr0=0, sXr1=0)

        def barrier():
            snap = dict(n)
            allsems = tuple(sems.keys())
            for dec, eng_sems in [
                (blk.sync, allsems),
                (blk.gpsimd, allsems),
                (blk.tensor, allsems),
                (blk.vector, allsems),
                (blk.scalar, allsems),
            ]:
                def _b(eng, names=eng_sems):
                    for nm in names:
                        if snap[nm]:
                            eng.wait_ge(sems[nm], snap[nm])
                dec(_b)

        @blk.sync
        def _(sync):
            for dst, src in [(c_convw, d_in["convw"]), (c_convb, d_in["convb"]),
                             (c_dtb, d_in["dtb"]), (c_D, d_in["dvec"]),
                             (c_A, d_in["alog"]), (c_gs, d_in["gs"]),
                             (c_us, d_in["us"]), (c_ds, d_in["ds"]),
                             (c_id, d_in["ident"])]:
                sync.dma_start(out=dst[:], in_=src).then_inc(sS, 16)
                n["sS"] += 16

        @blk.scalar
        def _(sc):
            sc.wait_ge(sS, n["sS"])
            sc.activation(c_A[:], c_A[:], AF.Exp).then_inc(sA, 1)
            n["sA"] += 1

        @blk.vector
        def _(v):
            v.memset(ones_row[:], 1.0)
            v.memset(c_row4[0:1, 0:1], EPS)
            v.memset(c_row4[0:1, 1:2], -0.5)
            v.memset(c_row4[0:1, 2:3], 1.0 / DM)
            v.memset(ones_bf[:], 1.0)
            v.memset(hprev[:], 0.0)
            v.memset(hist[:], 0.0)
            v.wait_ge(sA, n["sA"])
            v.tensor_scalar(c_A[:], c_A[:], -1.0, None, ALU.mult).then_inc(sV, 1)
            n["sV"] += 1

        barrier()

        dbg_marks = {}

        for ch in range(nch):
            _sid = nc.enter_named_scope(f"A{ch}", False)[0]
            phA = "A" in phases
            Ag = phA or "g" in phases
            As = phA or "s" in phases
            At = phA or "t" in phases
            Aa = phA or "a" in phases
            Av = phA or "v" in phases
            bG, bS, bT, bV, bA = n["sG"], n["sS"], n["sT"], n["sV"], n["sA"]
            bE = n["sE"]
            bGx, bGu, bGb = n["sGx"], n["sGu"], n["sGb"]
            bSl, bSx = n["sSl"], n["sSx"]
            bGs = [n[f"sGs{i}"] for i in range(4)]
            bSt = [n[f"sSt{i}"] for i in range(2)]
            G_xbf = bGx + 16
            G_xp1 = bGu + 32
            G_slab = lambda et: (et % 4, bGs[et % 4] + 16 * (et // 4 + 1))
            S_st = lambda et: (et % 2, bSt[et % 2] + 16 * (et // 2 + 1))
            S_bc = bS + 16
            T_ip = lambda et: bT + 3 * et + 1
            T_xp = lambda et: bT + 3 * et + 3
            A_ev = lambda et: bA + 3 * et + 1
            A_si = lambda et: bA + 3 * et + 2
            A_ct = lambda et: bA + 3 * et + 3
            V_cv = lambda et: bV + et + 1

            @blk.gpsimd
            def _(g, ch=ch, bG=bG):
                if not Ag: return
                g.dma_start(out=xbf[:].rearrange("p (k t) -> p k t", k=NKD),
                            in_=d_in["xt_host"][ch].rearrange("k p t -> p k t")
                            ).then_inc(sGx, 16)
                g.dma_start(out=u8[:].rearrange("p (e m) -> p e m", e=NET),
                            in_=d_in["wxp1"]).then_inc(sGu, 16)
                g.dma_start(out=w_xp2[:].rearrange("p (e m) -> p e m", e=NET),
                            in_=d_in["wxp2"]).then_inc(sGu, 16)
                slots4 = [wsA[:, 0:2048], wsA[:, 2048:4096],
                          wsB[:, 0:2048], wsB[:, 2048:4096]]
                for et in range(NET):
                    if et >= 4:
                        g.wait_ge(sT, T_ip(et - 4))
                    wsl = slots4[et % 4]
                    g.dma_start(out=wsl.rearrange("p (k m) -> p k m", k=NKD),
                                in_=d_in["winx"][et]).then_inc(sGs[et % 4], 16)
                for _i in range(4):
                    n[f"sGs{_i}"] += 16 * (NET // 4)
                n["sGx"] += 16
                n["sGu"] += 32

            @blk.sync
            def _(sync, ch=ch):
                if not As: return
                for et in range(NET):
                    sync.wait_ge(sA, A_si(et))
                    xt = xt0 if et % 2 == 0 else xt1
                    sync.dma_start(out=xt_scr[et], in_=xt[:]
                                   ).then_inc(sSt[et % 2], 16)
                sync.wait_ge(sV, bV + NET + 1)
                sync.dma_start(out=bc_scr, in_=bc32[:]).then_inc(sS, 16)
                n["sSt0"] += 16 * (NET // 2)
                n["sSt1"] += 16 * (NET // 2)
                n["sS"] += 16

            @blk.tensor
            def _(t, ch=ch):
                if not At: return
                t.wait_ge(sGx, G_xbf)
                t.wait_ge(sGu, G_xp1)
                slots4 = [wsA[:, 0:2048], wsA[:, 2048:4096],
                          wsB[:, 0:2048], wsB[:, 2048:4096]]
                for et in range(NET):
                    sl, v_ = G_slab(et)
                    t.wait_ge(sGs[sl], v_)
                    if et >= 3:
                        t.wait_ge(sA, A_ev(et - 3))
                    wsl = slots4[et % 4]
                    px = pb[[0, 1, 4][et % 3]]
                    if "ip" not in SKIP:
                        for k in range(NKD):
                            mm = t.matmul(px[:], wsl[:, k * P:(k + 1) * P],
                                          xbf[:, k * TC:(k + 1) * TC],
                                          start=(k == 0), stop=(k == NKD - 1))
                        mm.then_inc(sT, 1)
                    else:
                        t.nop().then_inc(sT, 1)
                    t.wait_ge(sA, A_ct(et))
                    xb = xtb0 if et % 2 == 0 else xtb1
                    if "xp" not in SKIP:
                        t.matmul(pb[2][:], u8[:, et * P:(et + 1) * P], xb[:],
                                 start=(et == 0), stop=(et == NET - 1)).then_inc(sT, 1)
                        t.matmul(pb[3][0:32, :], w_xp2[:, et * 32:(et + 1) * 32], xb[:],
                                 start=(et == 0), stop=(et == NET - 1)).then_inc(sT, 1)
                    else:
                        t.nop().then_inc(sT, 1)
                        t.nop().then_inc(sT, 1)
                t.matmul(pb[6][:, 0:P], ones_row[:], ones_row[:],
                         start=True, stop=True).then_inc(sT, 1)
                n["sT"] += 3 * NET + 1

            @blk.scalar
            def _(sc, ch=ch):
                if not Aa: return
                for et in range(NET):
                    sc.wait_ge(sT, T_ip(et))
                    if et >= 2:
                        sc.wait_ge(sV, V_cv(et - 2))
                        sc.wait_ge(sE, bE + 2 * et - 2)
                        _sl, _v = S_st(et - 2)
                        sc.wait_ge(sSt[_sl], _v)
                    cb = cb0 if et % 2 == 0 else cb1
                    px = pb[[0, 1, 4][et % 3]]
                    if "ev" not in SKIP:
                        sc.activation(cb[:, 3:3 + TC], px[:], AF.Copy).then_inc(sA, 1)
                    else:
                        sc.nop().then_inc(sA, 1)
                    sc.wait_ge(sV, V_cv(et))
                    xt = xt0 if et % 2 == 0 else xt1
                    if "si" not in SKIP:
                        sc.activation(xt[:], t2[:], AF.Silu,
                                      bias=c_convb[:, et:et + 1]).then_inc(sA, 1)
                    else:
                        sc.nop().then_inc(sA, 1)
                    if et >= 1:
                        sc.wait_ge(sT, T_xp(et - 1))
                    xb = xtb0 if et % 2 == 0 else xtb1
                    if "ct" not in SKIP:
                        sc.activation(xb[:], xt[:], AF.Copy).then_inc(sA, 1)
                    else:
                        sc.nop().then_inc(sA, 1)
                n["sA"] += 3 * NET
                sc.wait_ge(sT, bT + 3 * NET + 1)
                sc.activation(dtlow[:], pb[2][:], AF.Copy).then_inc(sA, 1)
                n["sA"] += 1

            @blk.vector
            def _(v, ch=ch):
                if not Av: return
                for et in range(NET):
                    cb = cb0 if et % 2 == 0 else cb1
                    if "hc" not in SKIP:
                        v.tensor_copy(cb[:, 0:3], hist[:, et * 3:(et + 1) * 3]
                                      ).then_inc(sE, 1)
                        v.wait_ge(sE, bE + 2 * et + 1)
                    v.wait_ge(sA, A_ev(et))
                    if "cv" not in SKIP:
                        v.tensor_scalar(t1[:], cb[:, 0:TC],
                                        c_convw[:, et * 4:et * 4 + 1], None, ALU.mult)
                        v.scalar_tensor_tensor(t2[:], cb[:, 1:1 + TC],
                                               c_convw[:, et * 4 + 1:et * 4 + 2],
                                               t1[:], ALU.mult, ALU.add)
                        v.scalar_tensor_tensor(t1[:], cb[:, 2:2 + TC],
                                               c_convw[:, et * 4 + 2:et * 4 + 3],
                                               t2[:], ALU.mult, ALU.add)
                        v.scalar_tensor_tensor(t2[:], cb[:, 3:3 + TC],
                                               c_convw[:, et * 4 + 3:et * 4 + 4],
                                               t1[:], ALU.mult, ALU.add).then_inc(sV, 1)
                    else:
                        v.nop().then_inc(sV, 1)
                    if "hc" not in SKIP:
                        v.tensor_copy(hist[:, et * 3:(et + 1) * 3], cb[:, TC:TC + 3]
                                      ).then_inc(sE, 1)
                    n["sE"] += 2
                n["sV"] += NET
                v.wait_ge(sT, bT + 3 * NET + 1)
                v.tensor_copy(bc32[:], pb[3][0:32, :]).then_inc(sV, 1)
                n["sV"] += 1

            barrier()

            nc.leave_named_scope(f"A{ch}", _sid, False)
            _sid = nc.enter_named_scope(f"B{ch}", False)[0]
            phB = "B" in phases
            bG, bS, bT, bV, bA = n["sG"], n["sS"], n["sT"], n["sV"], n["sA"]
            bGx, bGu, bGb = n["sGx"], n["sGu"], n["sGb"]
            bSl, bSx = n["sSl"], n["sSx"]
            bGs = [n[f"sGs{i}"] for i in range(4)]
            bLd = [n[f"sLd{i}"] for i in range(2)]
            G_wdt = bGu + 16
            G_cnm = bGb + 32
            G_zsl = lambda et: (et % 4, bGs[et % 4] + 16 * (et // 4 + 1))
            S_xt = lambda et: (et % 2, bLd[et % 2] + 16 * (et // 2 + 1))
            T_dt = lambda et: bT + 2 * et + 1
            T_z = lambda et: bT + 2 * et + 2
            A_sp = lambda et: bA + 19 * et + 1
            A_dA = lambda et, nn: bA + 19 * et + 2 + nn
            A_zr = lambda et: bA + 19 * et + 18
            A_sg = lambda et: bA + 19 * et + 19
            V_dtf = lambda et: bV + 1 if et == 0 else bV + 19 * et
            V_sc = lambda et, nn: bV + 19 * et + nn + 2
            V_zsp = lambda et: bV + 19 * et + 18
            V_y = lambda et: bV + 19 * et + 20

            @blk.gpsimd
            def _(g, ch=ch):
                if not phB: return
                g.dma_start(out=u8[:],
                            in_=d_in["wdt"].rearrange("r e m -> r (e m)")
                            ).then_inc(sGu, 16)
                g.wait_ge(sS, S_bc)
                g.dma_start(out=Bnm.rearrange("p (n t) -> p n t", n=NST),
                            in_=bc_scr[0:NST].unsqueeze(0)
                            .broadcast_to([P, NST, TC])).then_inc(sGb, 16)
                g.dma_start(out=Cnm.rearrange("p (n t) -> p n t", n=NST),
                            in_=bc_scr[NST:32].unsqueeze(0)
                            .broadcast_to([P, NST, TC])).then_inc(sGb, 16)
                slots4 = [wsA[:, 0:2048], wsA[:, 2048:4096],
                          wsB[:, 0:2048], wsB[:, 2048:4096]]
                for et in range(NET):
                    if et >= 4:
                        g.wait_ge(sT, T_z(et - 4))
                    wsl = slots4[et % 4]
                    g.dma_start(out=wsl.rearrange("p (k m) -> p k m", k=NKD),
                                in_=d_in["winz"][et]).then_inc(sGs[et % 4], 16)
                for _i in range(4):
                    n[f"sGs{_i}"] += 16 * (NET // 4)
                n["sGu"] += 16
                n["sGb"] += 32

            @blk.sync
            def _(sync, ch=ch):
                if not phB: return
                for et in range(NET):
                    if et >= 2:
                        sync.wait_ge(sV, V_y(et - 2))
                    xt = xt0 if et % 2 == 0 else xt1
                    sync.dma_start(out=xt[:], in_=xt_scr[et]
                                   ).then_inc(sLd[et % 2], 16)
                n["sLd0"] += 16 * (NET // 2)
                n["sLd1"] += 16 * (NET // 2)

            @blk.tensor
            def _(t, ch=ch):
                if not phB: return
                slots4 = [wsA[:, 0:2048], wsA[:, 2048:4096],
                          wsB[:, 0:2048], wsB[:, 2048:4096]]

                def zmm(k):
                    _sl, _v = G_zsl(k)
                    t.wait_ge(sGs[_sl], _v)
                    if k >= 2:
                        t.wait_ge(sV, V_y(k - 2))
                    wsl = slots4[k % 4]
                    pz = pb[k % 2]
                    for kk in range(NKD):
                        mm = t.matmul(pz[:], wsl[:, kk * P:(kk + 1) * P],
                                      xbf[:, kk * TC:(kk + 1) * TC],
                                      start=(kk == 0), stop=(kk == NKD - 1))
                    mm.then_inc(sT, 1)

                def dtmm(k):
                    if k >= 1:
                        t.wait_ge(sA, A_sp(k - 1))
                    t.matmul(pb[4][:], u8[:, k * P:(k + 1) * P], dtlow[:],
                             start=True, stop=True).then_inc(sT, 1)

                t.wait_ge(sGu, G_wdt)
                dtmm(0)
                zmm(0)
                for k in range(NET - 1):
                    dtmm(k + 1)
                    zmm(k + 1)
                n["sT"] += 2 * NET

            @blk.scalar
            def _(sc, ch=ch):
                if not phB: return

                def sp_block(k):
                    bcol = c_dtb[:, k:k + 1]
                    sc.wait_ge(sT, T_dt(k))
                    if k >= 1:
                        sc.wait_ge(sV, V_zsp(k - 1))
                    sc.activation(t1[:], pb[4][:], AF.Abs, bias=bcol)
                    sc.activation(t2[:], t1[:], AF.Exp, scale=-1.0)
                    sc.activation(t1[:], t2[:], AF.Ln, bias=1.0)
                    sc.activation(t2[:], pb[4][:], AF.Relu, bias=bcol
                                  ).then_inc(sA, 1)

                sp_block(0)
                for k in range(NET):
                    dtf = dt_f if k % 2 == 0 else dt_f1
                    for nn in range(NST):
                        if k == 0 and nn == 0:
                            sc.wait_ge(sV, V_dtf(0))
                        elif nn == 0:
                            sc.wait_ge(sV, V_dtf(k))
                        if nn >= 2:
                            sc.wait_ge(sV, V_sc(k, nn - 2))
                        dA = dA0 if nn % 2 == 0 else dA1
                        sc.activation(dA[:], dtf[:], AF.Exp,
                                      scale=c_A[:, k * NST + nn:k * NST + nn + 1]
                                      ).then_inc(sA, 1)
                    pz = pb[k % 2]
                    sc.wait_ge(sT, T_z(k))
                    sc.wait_ge(sV, V_dtf(k))
                    sc.activation(t1[:], pz[:], AF.Abs)
                    sc.activation(t2[:], t1[:], AF.Exp, scale=-1.0)
                    sc.activation(t1[:], t2[:], AF.Ln, bias=1.0)
                    sc.activation(t2[:], pz[:], AF.Relu, scale=-1.0).then_inc(sA, 1)
                    if k >= 1:
                        sc.wait_ge(sV, V_y(k - 1))
                    sc.wait_ge(sV, V_zsp(k))
                    sc.activation(sg[:], t1[:], AF.Exp, scale=-1.0).then_inc(sA, 1)
                    if k < NET - 1:
                        sp_block(k + 1)
                    else:
                        sc.nop().then_inc(sA, 1)
                n["sA"] += 19 * NET + 1

            @blk.vector
            def _(v, ch=ch):
                if not phB: return
                htn3 = htn.rearrange("p (t n) -> p t n", n=NST)

                def dtf_du(k):
                    dtf = dt_f if k % 2 == 0 else dt_f1
                    dup = du if k % 2 == 0 else du1
                    v.wait_ge(sA, A_sp(k))
                    v.tensor_add(dtf[:], t1[:], t2[:]).then_inc(sV, 1)
                    _sl, _v = S_xt(k)
                    v.wait_ge(sLd[_sl], _v)
                    xt = xt0 if k % 2 == 0 else xt1
                    v.tensor_mul(dup[:], dtf[:], xt[:])

                v.wait_ge(sGb, G_cnm)
                dtf_du(0)
                for k in range(NET):
                    pz = pb[k % 2]
                    dup = du if k % 2 == 0 else du1
                    xt = xt0 if k % 2 == 0 else xt1
                    for nn in range(NST):
                        dA = dA0 if nn % 2 == 0 else dA1
                        wbuf = wb0 if nn % 2 == 0 else wb1
                        v.tensor_mul(wbuf[:], dup[:],
                                     Bnm[:, nn * TC:(nn + 1) * TC])
                        v.wait_ge(sA, A_dA(k, nn))
                        v.tensor_tensor_scan(
                            htn3[:, :, nn], dA[:], wbuf[:],
                            hprev[:, k * NST + nn:k * NST + nn + 1],
                            ALU.mult, ALU.add).then_inc(sV, 1)
                    v.wait_ge(sA, A_zr(k))
                    v.tensor_add(t1[:], t1[:], t2[:]).then_inc(sV, 1)
                    if k < NET - 1:
                        dtf_du(k + 1)
                    else:
                        v.nop().then_inc(sV, 1)
                    v.wait_ge(sV, V_sc(k, NST - 1))
                    v.tensor_copy(hprev[:, k * NST:(k + 1) * NST],
                                  htn[:, (TC - 1) * NST:TC * NST])
                    v.tensor_tensor(htn3[:, :, :], htn3[:, :, :],
                                    Cnm.rearrange("p (n t) -> p t n", n=NST),
                                    ALU.mult)
                    v.tensor_reduce(yred[:], htn3[:, :, :], AX.X, ALU.add)
                    v.scalar_tensor_tensor(yred[:], xt[:], c_D[:, k:k + 1],
                                           yred[:], ALU.mult, ALU.add)
                    v.wait_ge(sA, A_sg(k))
                    v.tensor_mul(yred[:], yred[:], sg[:])
                    v.tensor_tensor(ybf[:, k * TC:(k + 1) * TC], yred[:], pz[:],
                                    ALU.mult).then_inc(sV, 1)
                n["sV"] += 19 * NET + 1

            if debug and phB:
                @blk.sync
                def _(sync, ch=ch):
                    sync.wait_ge(sV, n["sV"])
                    sync.wait_ge(sA, n["sA"])
                    sync.dma_start(out=dbg["dbg_dt"][ch], in_=dt_f[:]).then_inc(sS, 16)
                    sync.dma_start(out=dbg["dbg_du"][ch], in_=du[:]).then_inc(sS, 16)
                    sync.dma_start(out=dbg["dbg_yred"][ch], in_=yred[:]).then_inc(sS, 16)
                    xt = xt0 if (NET - 1) % 2 == 0 else xt1
                    sync.dma_start(out=dbg["dbg_xt"][ch], in_=xt[:]).then_inc(sS, 16)
                    sync.dma_start(out=dbg["dbg_proj"][ch], in_=bc32[:]).then_inc(sS, 16)
                    sync.dma_start(out=dbg["dbg_hprev"][ch], in_=hprev[:]).then_inc(sS, 16)
                    n["sS"] += 16 * 6

            barrier()

            nc.leave_named_scope(f"B{ch}", _sid, False)
            _sid = nc.enter_named_scope(f"C{ch}", False)[0]
            phC = "C" in phases
            bG, bS, bT, bV, bA = n["sG"], n["sS"], n["sT"], n["sV"], n["sA"]
            bGx, bGu, bGb = n["sGx"], n["sGu"], n["sGb"]
            bSl, bSx = n["sSl"], n["sSx"]
            bGs = [n[f"sGs{i}"] for i in range(4)]
            bXr = [n[f"sXr{i}"] for i in range(2)]
            G_wo = lambda dt: (dt % 2, bGs[dt % 2] + 16 * (dt // 2 + 1))
            S_xr = lambda dt: (dt % 2, bXr[dt % 2] + 16 * (dt // 2 + 1))
            T_mm = lambda dt: bT + dt + 2 * max(0, dt - 2) + 1
            T_st = bT + 3 * NDT
            V_stt = lambda dt: bV + dt + 1
            A_c = lambda dt: bA + 2 * (dt + 1)

            @blk.gpsimd
            def _(g, ch=ch):
                if not phC: return
                for dt in range(NDT):
                    if dt >= 2:
                        g.wait_ge(sT, T_mm(dt - 2))
                    wsl = (wsA if dt % 2 == 0 else wsB)[:, 0:NET * P]
                    g.dma_start(out=wsl.rearrange("p (k m) -> p k m", k=NET),
                                in_=d_in["wout"][dt]).then_inc(sGs[dt % 2], 16)
                n["sGs0"] += 16 * (NDT // 2)
                n["sGs1"] += 16 * (NDT // 2)

            @blk.sync
            def _(sync, ch=ch):
                if not phC: return
                for dt in range(NDT):
                    if dt >= 2:
                        sync.wait_ge(sV, V_stt(dt - 2))
                    xr = xt0 if dt % 2 == 0 else xt1
                    sync.dma_start(out=xr[:], in_=d_in["xt_host"][ch][dt]
                                   ).then_inc(sXr[dt % 2], 16)
                n["sXr0"] += 16 * (NDT // 2)
                n["sXr1"] += 16 * (NDT // 2)

            @blk.tensor
            def _(t, ch=ch):
                if not phC: return
                for dt in range(NDT):
                    _sl, _v = G_wo(dt)
                    t.wait_ge(sGs[_sl], _v)
                    if dt >= 2:
                        t.wait_ge(sV, V_stt(dt - 2))
                    wsl = (wsA if dt % 2 == 0 else wsB)[:, 0:NET * P]
                    po = pb[dt % 2]
                    for k in range(NET):
                        mm = t.matmul(po[:], wsl[:, k * P:(k + 1) * P],
                                      ybf[:, k * TC:(k + 1) * TC],
                                      start=(k == 0), stop=(k == NET - 1))
                    mm.then_inc(sT, 1)
                    if dt >= 2:
                        t.wait_ge(sA, A_c(dt - 2))
                        t.matmul(pb[2][0:1, :], ones_bf[:], xtb0[:],
                                 start=(dt - 2 == 0), stop=(dt - 2 == NDT - 1)
                                 ).then_inc(sT, 1)
                        t.matmul(pb[3][0:1, :], ones_bf[:], xtb1[:],
                                 start=(dt - 2 == 0), stop=(dt - 2 == NDT - 1)
                                 ).then_inc(sT, 1)
                for dtl in range(NDT - 2, NDT):
                    t.wait_ge(sA, A_c(dtl))
                    t.matmul(pb[2][0:1, :], ones_bf[:], xtb0[:],
                             start=(dtl == 0), stop=(dtl == NDT - 1)).then_inc(sT, 1)
                    t.matmul(pb[3][0:1, :], ones_bf[:], xtb1[:],
                             start=(dtl == 0), stop=(dtl == NDT - 1)).then_inc(sT, 1)
                t.wait_ge(sA, bA + 2 * NDT + 2)
                t.matmul(pb[4][:], ones_row[:], srow_b, start=True,
                         stop=True).then_inc(sT, 1)
                t.wait_ge(sV, bV + NDT + 5)
                t.matmul(pb[5][:], ones_row[:], trow, start=True,
                         stop=True).then_inc(sT, 1)
                n["sT"] += 3 * NDT + 2

            @blk.scalar
            def _(sc, ch=ch):
                if not phC: return
                for dt in range(NDT):
                    sc.wait_ge(sV, V_stt(dt))
                    if dt >= 1:
                        j = dt - 1
                        pos = 3 * j + 5 if j <= NDT - 3 else 3 * NDT - 2 * (NDT - 1 - j)
                        sc.wait_ge(sT, bT + pos)
                    sc.activation(xtb0[:], x_res[:, dt * TC:(dt + 1) * TC],
                                  AF.Copy).then_inc(sA, 1)
                    sc.activation(xtb1[:], x_res[:, dt * TC:(dt + 1) * TC],
                                  AF.Square).then_inc(sA, 1)
                n["sA"] += 2 * NDT
                sc.wait_ge(sV, bV + NDT + 4)
                sc.activation(trow, srow_b, AF.Ln, bias=c_row4[0:1, 0:1]).then_inc(sA, 1)
                sc.activation(srow_b, trow, AF.Exp, scale=c_row4[0:1, 1:2]).then_inc(sA, 1)
                n["sA"] += 2
                for dt in range(NDT):
                    sc.wait_ge(sV, bV + NDT + 5 + 2 * (dt + 1))
                    sc.activation(x1b[:, dt * TC:(dt + 1) * TC],
                                  x1[:, dt * TC:(dt + 1) * TC], AF.Copy
                                  ).then_inc(sA, 1)
                n["sA"] += NDT

            @blk.vector
            def _(v, ch=ch):
                if not phC: return
                for dt in range(NDT):
                    v.wait_ge(sT, T_mm(dt))
                    _sl, _v = S_xr(dt)
                    v.wait_ge(sXr[_sl], _v)
                    po = pb[dt % 2]
                    xr = xt0 if dt % 2 == 0 else xt1
                    v.scalar_tensor_tensor(x_res[:, dt * TC:(dt + 1) * TC],
                                           po[:], 1.0, xr[:], ALU.mult,
                                           ALU.add).then_inc(sV, 1)
                v.wait_ge(sT, T_st)
                v.tensor_scalar(srow_a, pb[2][0:1, :], c_row4[0:1, 2:3], None, ALU.mult).then_inc(sV, 1)
                v.tensor_scalar(srow_b, pb[3][0:1, :], c_row4[0:1, 2:3], None, ALU.mult).then_inc(sV, 1)
                v.tensor_mul(trow, srow_a, srow_a).then_inc(sV, 1)
                v.tensor_sub(srow_b, srow_b, trow).then_inc(sV, 1)
                v.wait_ge(sA, bA + 2 * NDT + 2)
                v.tensor_mul(trow, srow_a, srow_b).then_inc(sV, 1)
                n["sV"] += NDT + 5
                v.wait_ge(sT, bT + 3 * NDT + 2)
                for dt in range(NDT):
                    v.tensor_tensor(x1[:, dt * TC:(dt + 1) * TC],
                                    x_res[:, dt * TC:(dt + 1) * TC],
                                    pb[4][:], ALU.mult).then_inc(sV, 1)
                    v.tensor_sub(x1[:, dt * TC:(dt + 1) * TC],
                                 x1[:, dt * TC:(dt + 1) * TC],
                                 pb[5][:]).then_inc(sV, 1)
                n["sV"] += 2 * NDT

            if debug and phC:
                @blk.sync
                def _(sync, ch=ch):
                    sync.wait_ge(sV, n["sV"])
                    sync.dma_start(out=dbg["dbg_x1"][ch],
                                   in_=x1[:, 0:TC]).then_inc(sS, 16)
                    n["sS"] += 16

            barrier()

            nc.leave_named_scope(f"C{ch}", _sid, False)
            _sid = nc.enter_named_scope(f"D{ch}", False)[0]
            phD = "D" in phases
            bG, bS, bT, bV, bA = n["sG"], n["sS"], n["sT"], n["sV"], n["sA"]
            bGs = [n[f"sGs{i}"] for i in range(4)]
            G_g = lambda it: (it % 2, bGs[it % 2] + 16 * (it // 2 + 1))
            G_u = lambda it: (2 + it % 2, bGs[2 + it % 2] + 16 * (it // 2 + 1))
            _ng = [NIT - NIT // 2, NIT // 2]
            def G_d(dt, q):
                idx = dt * 4 + q
                s = idx % 4
                prior = (NIT // 2 + NIT % 2) if s == 0 else (NIT // 2)
                if s >= 2:
                    prior = (NIT // 2 + NIT % 2) if s == 2 else (NIT // 2)
                return (s, bGs[s] + 16 * (prior + idx // 4 + 1))
            T_g = lambda it: bT + 2 * it + 1
            T_u = lambda it: bT + 2 * it + 2
            T_dnq = lambda dt, q: bT + 2 * NIT + 4 * dt + q + 1
            T_dn = lambda dt: bT + 2 * NIT + 4 * dt + 4
            T_st2 = bT + 2 * NIT + 4 * NDT + 2 * NDT
            T_bc2 = T_st2 + 2
            T_tr = lambda dt, j: T_bc2 + 4 * dt + j + 1
            A_sil = lambda it: bA + it + 1
            V_hdn = lambda it: bV + it + 1
            V_x2 = lambda dt: bV + NIT + dt + 1
            A_cst = lambda dt: bA + NIT + 2 * (dt + 1)
            V_nr = lambda dt: bV + NIT + NDT + 5 + dt + 1
            A_ev2 = lambda dt, j: bA + NIT + 2 * NDT + 2 + 4 * dt + j + 1

            gsl = [wsA[:, 0:2048], wsB[:, 0:2048]]
            usl = [wsA[:, 2048:4096], wsB[:, 2048:4096]]
            dsl = [wsA[:, 0:1408], wsA[:, 1408:2816],
                   wsB[:, 0:1408], wsB[:, 1408:2816]]

            @blk.gpsimd
            def _(g, ch=ch):
                if not phD: return
                for it in range(NIT):
                    if it >= 2:
                        g.wait_ge(sT, T_u(it - 2))
                    g.dma_start(out=gsl[it % 2].rearrange("p (k m) -> p k m", k=NKD),
                                in_=d_in["wg"][it]).then_inc(sGs[it % 2], 16)
                    g.dma_start(out=usl[it % 2].rearrange("p (k m) -> p k m", k=NKD),
                                in_=d_in["wu"][it]).then_inc(sGs[2 + it % 2], 16)
                for dt in range(NDT):
                    for q in range(4):
                        idx = dt * 4 + q
                        if idx >= 4:
                            g.wait_ge(sT, T_dnq((idx - 4) // 4, (idx - 4) % 4))
                        elif idx == 0:
                            g.wait_ge(sT, T_u(NIT - 1))
                        g.dma_start(out=dsl[idx % 4]
                                    .rearrange("p (k m) -> p k m", k=11),
                                    in_=d_in["wd"][dt, :, q * 11:(q + 1) * 11, :]
                                    ).then_inc(sGs[idx % 4], 16)
                n["sGs0"] += 16 * (NIT - NIT // 2 + NDT)
                n["sGs1"] += 16 * (NIT // 2 + NDT)
                n["sGs2"] += 16 * (NIT - NIT // 2 + NDT)
                n["sGs3"] += 16 * (NIT // 2 + NDT)

            @blk.tensor
            def _(t, ch=ch):
                if not phD: return
                for it in range(NIT):
                    _sl, _v = G_g(it)
                    t.wait_ge(sGs[_sl], _v)
                    if it >= 2:
                        t.wait_ge(sA, A_sil(it - 2))
                        t.wait_ge(sV, V_hdn(it - 2))
                    pg = pb[it % 2]
                    for k in range(NKD):
                        mm = t.matmul(pg[:], gsl[it % 2][:, k * P:(k + 1) * P],
                                      x1b[:, k * TC:(k + 1) * TC],
                                      start=(k == 0), stop=(k == NKD - 1))
                    mm.then_inc(sT, 1)
                    _sl, _v = G_u(it)
                    t.wait_ge(sGs[_sl], _v)
                    pu = pb[2 + it % 2]
                    for k in range(NKD):
                        mm = t.matmul(pu[:], usl[it % 2][:, k * P:(k + 1) * P],
                                      x1b[:, k * TC:(k + 1) * TC],
                                      start=(k == 0), stop=(k == NKD - 1))
                    mm.then_inc(sT, 1)
                for dt in range(NDT):
                    if dt == 0:
                        t.wait_ge(sV, V_hdn(NIT - 1))
                    if dt >= 2:
                        t.wait_ge(sV, V_x2(dt - 2))
                    pd = pb[dt % 2]
                    for q in range(4):
                        idx = dt * 4 + q
                        _sl, _v = G_d(dt, q)
                        t.wait_ge(sGs[_sl], _v)
                        for kk in range(11):
                            k = q * 11 + kk
                            mm = t.matmul(pd[:], dsl[idx % 4][:, kk * P:(kk + 1) * P],
                                          hdn[:, k * TC:(k + 1) * TC],
                                          start=(k == 0), stop=(k == NIT - 1))
                        mm.then_inc(sT, 1)
                for dt in range(NDT):
                    t.wait_ge(sA, A_cst(dt))
                    t.matmul(pb[2][0:1, :], ones_bf[:], xtb0[:],
                             start=(dt == 0), stop=(dt == NDT - 1)).then_inc(sT, 1)
                    t.matmul(pb[3][0:1, :], ones_bf[:], xtb1[:],
                             start=(dt == 0), stop=(dt == NDT - 1)).then_inc(sT, 1)
                t.wait_ge(sA, bA + NIT + 2 * NDT + 2)
                t.matmul(pb[4][:], ones_row[:], srow_b, start=True,
                         stop=True).then_inc(sT, 1)
                t.wait_ge(sV, bV + NIT + NDT + 5)
                t.matmul(pb[5][:], ones_row[:], trow, start=True,
                         stop=True).then_inc(sT, 1)
                for dt in range(NDT):
                    t.wait_ge(sV, V_nr(dt))
                    for j in range(4):
                        idx = dt * 4 + j
                        pt = pb[6 + idx % 2]
                        if idx >= 2:
                            t.wait_ge(sA, A_ev2((idx - 2) // 4, (idx - 2) % 4))
                        t.transpose(pt[:, 0:P], xt0[:, j * P:(j + 1) * P],
                                    c_id[:]).then_inc(sT, 1)
                n["sT"] += 2 * NIT + 4 * NDT + 2 * NDT + 2 + 4 * NDT

            @blk.scalar
            def _(sc, ch=ch):
                if not phD: return
                for it in range(NIT):
                    sc.wait_ge(sT, T_g(it))
                    if it >= 2:
                        sc.wait_ge(sV, V_hdn(it - 2))
                    sg = t1 if it % 2 == 0 else t2
                    sc.activation(sg[:], pb[it % 2][:], AF.Silu,
                                  scale=c_gs[:, it:it + 1]).then_inc(sA, 1)
                n["sA"] += NIT
                for dt in range(NDT):
                    sc.wait_ge(sV, V_x2(dt))
                    if dt >= 1:
                        sc.wait_ge(sT, bT + 2 * NIT + 4 * NDT + 2 * dt)
                    sc.activation(xtb0[:], x1[:, dt * TC:(dt + 1) * TC],
                                  AF.Copy).then_inc(sA, 1)
                    sc.activation(xtb1[:], x1[:, dt * TC:(dt + 1) * TC],
                                  AF.Square).then_inc(sA, 1)
                n["sA"] += 2 * NDT
                sc.wait_ge(sV, bV + NIT + NDT + 4)
                sc.activation(trow, srow_b, AF.Ln, bias=c_row4[0:1, 0:1]).then_inc(sA, 1)
                sc.activation(srow_b, trow, AF.Exp, scale=c_row4[0:1, 1:2]).then_inc(sA, 1)
                n["sA"] += 2
                for dt in range(NDT):
                    for j in range(4):
                        idx = dt * 4 + j
                        sc.wait_ge(sT, T_tr(dt, j))
                        pt = pb[6 + idx % 2]
                        sc.activation(obufs[j][:, dt * P:(dt + 1) * P],
                                      pt[:, 0:P], AF.Copy).then_inc(sA, 1)
                n["sA"] += 4 * NDT

            @blk.vector
            def _(v, ch=ch):
                if not phD: return
                for it in range(NIT):
                    v.wait_ge(sT, T_u(it))
                    v.wait_ge(sA, A_sil(it))
                    sg = t1 if it % 2 == 0 else t2
                    v.scalar_tensor_tensor(hdn[:, it * TC:(it + 1) * TC],
                                           pb[2 + it % 2][:], c_us[:, it:it + 1],
                                           sg[:], ALU.mult, ALU.mult
                                           ).then_inc(sV, 1)
                for dt in range(NDT):
                    v.wait_ge(sT, T_dn(dt))
                    v.scalar_tensor_tensor(x1[:, dt * TC:(dt + 1) * TC],
                                           pb[dt % 2][:], c_ds[:, dt:dt + 1],
                                           x1[:, dt * TC:(dt + 1) * TC],
                                           ALU.mult, ALU.add).then_inc(sV, 1)
                v.wait_ge(sT, T_st2)
                v.tensor_scalar(srow_a, pb[2][0:1, :], c_row4[0:1, 2:3], None, ALU.mult).then_inc(sV, 1)
                v.tensor_scalar(srow_b, pb[3][0:1, :], c_row4[0:1, 2:3], None, ALU.mult).then_inc(sV, 1)
                v.tensor_mul(trow, srow_a, srow_a).then_inc(sV, 1)
                v.tensor_sub(srow_b, srow_b, trow).then_inc(sV, 1)
                v.wait_ge(sA, bA + NIT + 2 * NDT + 2)
                v.tensor_mul(trow, srow_a, srow_b).then_inc(sV, 1)
                n["sV"] += NIT + NDT + 5
                v.wait_ge(sT, T_bc2)
                for dt in range(NDT):
                    if dt >= 1:
                        v.wait_ge(sT, T_tr(dt - 1, 3))
                    v.tensor_tensor(xt0[:], x1[:, dt * TC:(dt + 1) * TC],
                                    pb[4][:], ALU.mult)
                    v.tensor_sub(xt0[:], xt0[:], pb[5][:]).then_inc(sV, 1)
                n["sV"] += NDT

            @blk.sync
            def _(sync, ch=ch):
                if not phD: return
                sync.wait_ge(sA, bA + NIT + 2 * NDT + 2 + 4 * NDT)
                for j in range(4):
                    sync.dma_start(out=out_d[ch * TC + j * P: ch * TC + (j + 1) * P, :],
                                   in_=obufs[j][:]).then_inc(sS, 16)
                n["sS"] += 64

            if debug and phD:
                @blk.sync
                def _(sync, ch=ch):
                    sync.wait_ge(sV, n["sV"])
                    sync.dma_start(out=dbg["dbg_y"][ch],
                                   in_=x1[:, 0:TC]).then_inc(sS, 16)
                    n["sS"] += 16

            barrier()

            nc.leave_named_scope(f"D{ch}", _sid, False)
        @blk.gpsimd
        def _(g):
            g.wait_ge(sS, n["sS"])

        @blk.sync
        def _(sync):
            sync.wait_ge(sS, n["sS"])

    return nc


_PROG_CACHE = {}


def _get_program(debug=False):
    key = bool(debug)
    if key not in _PROG_CACHE:
        _PROG_CACHE[key] = build_program(debug=debug)
    return _PROG_CACHE[key]


def kernel(**inputs):
    w = _host_prep(inputs)
    x = np.asarray(inputs["L0_attn_hidden_states"], np.float32)
    nc = _get_program(debug=False)
    in_maps = []
    for c in range(8):
        m = dict(w)
        m["xt_host"] = _host_x(x[c])
        in_maps.append(m)
    res = run_bass_kernel_spmd(nc, in_maps, core_ids=list(range(8)))
    return np.stack([r["out"] for r in res.results], axis=0)


# version 05
# speedup vs baseline: 1.0984x; 1.0790x over previous
import numpy as np
import concourse.bass as bass
import concourse.mybir as mybir
from concourse.bass_utils import run_bass_kernel_spmd

F32 = mybir.dt.float32
BF16 = mybir.dt.bfloat16
AF = mybir.ActivationFunctionType
ALU = mybir.AluOpType
AX = mybir.AxisListType

P = 128
L = 2048
DM = 2048
E = 4096
NST = 16
RK = 128
IT = 5632
TC = 512
NCH = L // TC
NET = E // P
NKD = DM // P
NDT = DM // P
NIT = IT // P
EPS = 1e-5
SKIP = set()
STRICT = set()


def _host_prep(inp):
    c = np.ascontiguousarray
    w = {}
    ipw = inp["L0_attn_in_proj_w"].astype(np.float32)
    w["winx"] = c(ipw[:E].reshape(NET, P, NKD, P).transpose(0, 3, 2, 1))
    w["winz"] = c(ipw[E:].reshape(NET, P, NKD, P).transpose(0, 3, 2, 1))
    xpw = inp["L0_attn_x_proj_w"].astype(np.float32)
    w["wxp1"] = c(xpw[:RK].T.reshape(NET, P, RK).transpose(1, 0, 2))
    w["wxp2"] = c(xpw[RK:].T.reshape(NET, P, 32).transpose(1, 0, 2))
    dtw = inp["L0_attn_dt_proj_w"].astype(np.float32)
    w["wdt"] = c(dtw.T.reshape(RK, NET, P))
    opw = inp["L0_attn_out_proj_w"].astype(np.float32)
    w["wout"] = c(opw.reshape(NDT, P, NET, P).transpose(0, 3, 2, 1))
    gw = inp["L0_ffn_gate_w"].astype(np.float32)
    w["wg"] = c(gw.reshape(NIT, P, NKD, P).transpose(0, 3, 2, 1))
    uw = inp["L0_ffn_up_w"].astype(np.float32)
    w["wu"] = c(uw.reshape(NIT, P, NKD, P).transpose(0, 3, 2, 1))
    dw = inp["L0_ffn_down_w"].astype(np.float32)
    w["wd"] = c(dw.reshape(NDT, P, NIT, P).transpose(0, 3, 2, 1))
    v32 = lambda v: c(v.astype(np.float32).reshape(-1, P).T)
    w["convw"] = c(inp["L0_attn_conv_w"].astype(np.float32)
                   .reshape(NET, P, 4).transpose(1, 0, 2).reshape(P, NET * 4))
    w["convb"] = v32(inp["L0_attn_conv_b"])
    w["dtb"] = v32(inp["L0_attn_dt_proj_b"])
    w["dvec"] = v32(inp["L0_attn_D"])
    w["alog"] = c(inp["L0_attn_A_log"].astype(np.float32)
                  .reshape(NET, P, NST).transpose(1, 0, 2).reshape(P, NET * NST))
    w["gs"] = v32(inp["L0_ffn_gate_s"])
    w["us"] = v32(inp["L0_ffn_up_s"])
    w["ds"] = v32(inp["L0_ffn_down_s"])
    w["ident"] = np.eye(P, dtype=np.float32)
    return w


def _host_x(x):
    return np.ascontiguousarray(
        x.astype(np.float32).reshape(NCH, TC, NKD, P).transpose(0, 2, 3, 1))


def build_program(debug=False, nch=NCH, phases="ABCD"):
    nc = bass.Bass("TRN2", target_bir_lowering=False)
    dt_ = nc.dram_tensor
    d_in = {}
    for name, shape in [
        ("winx", [NET, P, NKD, P]), ("winz", [NET, P, NKD, P]),
        ("wxp1", [P, NET, RK]), ("wxp2", [P, NET, 32]), ("wdt", [RK, NET, P]),
        ("wout", [NDT, P, NET, P]), ("wg", [NIT, P, NKD, P]),
        ("wu", [NIT, P, NKD, P]), ("wd", [NDT, P, NIT, P]),
        ("convw", [P, NET * 4]), ("convb", [P, NET]), ("dtb", [P, NET]),
        ("dvec", [P, NET]), ("alog", [P, NET * NST]),
        ("gs", [P, NIT]), ("us", [P, NIT]), ("ds", [P, NDT]),
        ("ident", [P, P]), ("xt_host", [NCH, NKD, P, TC]),
    ]:
        d_in[name] = dt_(name, shape, F32, kind="ExternalInput").ap()
    out_d = dt_("out", [L, DM], F32, kind="ExternalOutput").ap()
    xt_scr = dt_("xt_scr", [NET, P, TC], F32, kind="Internal").ap()
    bc_scr = dt_("bc_scr", [32, TC], F32, kind="Internal").ap()
    dbg = {}
    if debug:
        for name, shape in [("dbg_xt", [NCH, P, TC]), ("dbg_dt", [NCH, P, TC]),
                            ("dbg_yred", [NCH, P, TC]), ("dbg_du", [NCH, P, TC]),
                            ("dbg_x1", [NCH, P, TC]), ("dbg_proj", [NCH, 32, TC]),
                            ("dbg_y", [NCH, P, TC]), ("dbg_sg", [NCH, P, TC]),
                            ("dbg_hprev", [NCH, P, 512]), ("dbg_dtlow", [NCH, P, TC])]:
            dbg[name] = dt_(name, shape, F32, kind="ExternalOutput").ap()

    from contextlib import ExitStack
    es = ExitStack()
    sb = lambda n, s, d: es.enter_context(nc.sbuf_tensor(n, s, d))
    ps = lambda n, s: es.enter_context(nc.psum_tensor(n, s, F32))
    with es:
        c_convw = sb("c_convw", [P, NET * 4], F32)
        c_convb = sb("c_convb", [P, NET], F32)
        c_dtb = sb("c_dtb", [P, NET], F32)
        c_D = sb("c_D", [P, NET], F32)
        c_A = sb("c_A", [P, NET * NST], F32)
        c_gs = sb("c_gs", [P, NIT], F32)
        c_us = sb("c_us", [P, NIT], F32)
        c_ds = sb("c_ds", [P, NDT], F32)
        ones_bf = sb("ones_bf", [P, 1], BF16)
        ones_row = sb("ones_row", [1, P], F32)
        c_row4 = sb("c_row4", [1, 4], F32)
        c_id = sb("c_id", [P, P], F32)
        u8 = sb("u8", [P, NET * P], BF16)
        w_xp2 = sb("w_xp2", [P, NET * 32], BF16)
        hist = sb("hist", [P, NET * 3], F32)
        hprev = sb("hprev", [P, NET * NST], F32)
        cb0 = sb("cb0", [P, TC + 4], F32)
        cb1 = sb("cb1", [P, TC + 4], F32)
        xt0 = sb("xt0", [P, TC], F32)
        xt1 = sb("xt1", [P, TC], F32)
        xtb2 = sb("xtb2", [P, TC], F32)
        _xtbv = xtb2[:].bitcast(BF16)
        xtb0 = _xtbv[:, 0:TC]
        xtb1 = _xtbv[:, TC:2 * TC]
        dtlow = sb("dtlow", [P, TC], BF16)
        bc32 = sb("bc32", [32, TC], F32)
        dt_f = sb("dt_f", [P, TC], F32)
        dt_f1 = xtb2
        du = sb("du", [P, TC], F32)
        t1 = sb("t1", [P, TC], F32)
        t2 = sb("t2", [P, TC], F32)
        dA0 = sb("dA0", [P, TC], F32)
        dA1 = sb("dA1", [P, TC], F32)
        wb0 = sb("wb0", [P, TC], F32)
        wb1 = sb("wb1", [P, TC], F32)
        yred = sb("yred", [P, TC], F32)
        wsA = sb("wsA", [P, 4096], BF16)
        wsB = sb("wsB", [P, 4096], BF16)
        xbf = sb("xbf", [P, NKD * TC], BF16)
        ar1 = sb("ar1", [P, 16384], F32)
        ar2 = sb("ar2", [P, 16384], F32)

        Bnm = ar1[:, 0:8192]
        Cnm = ar1[:, 8192:16384]
        x1 = ar1[:, 0:8192]
        x1b = ar1[:, 8192:12288].bitcast(BF16)
        ob0 = ar1[:, 12288:14336]
        ob1 = ar1[:, 14336:16384]
        htn = ar2[:, 0:8192]
        ybf = ar2[:, 8192:16384].bitcast(BF16)
        x_res = ar2[:, 0:8192]
        hdn = ar2[:, 0:11264].bitcast(BF16)
        ob2 = ar2[:, 11264:13312]
        ob3 = ar2[:, 13312:15360]
        obufs = [ob0, ob1, ob2, ob3]
        du1 = cb0[:, 0:TC]
        sg = cb1[:, 0:TC]
        srow_a = du[0:1, :]
        srow_b = yred[0:1, :]
        trow = t1[0:1, :]

        pb = [ps(f"pb{i}", [P, TC]) for i in range(8)]

        sG = es.enter_context(nc.semaphore("sG"))
        sS = es.enter_context(nc.semaphore("sS"))
        sT = es.enter_context(nc.semaphore("sT"))
        sV = es.enter_context(nc.semaphore("sV"))
        sA = es.enter_context(nc.semaphore("sA"))
        sE = es.enter_context(nc.semaphore("sE"))
        sGx = es.enter_context(nc.semaphore("sGx"))
        sGu = es.enter_context(nc.semaphore("sGu"))
        sGb = es.enter_context(nc.semaphore("sGb"))
        sSl = es.enter_context(nc.semaphore("sSl"))
        sSx = es.enter_context(nc.semaphore("sSx"))
        sGs = [es.enter_context(nc.semaphore(f"sGs{i}")) for i in range(4)]
        sSt = [es.enter_context(nc.semaphore(f"sSt{i}")) for i in range(2)]
        sLd = [es.enter_context(nc.semaphore(f"sLd{i}")) for i in range(2)]
        sXr = [es.enter_context(nc.semaphore(f"sXr{i}")) for i in range(2)]
        sems = dict(sG=sG, sS=sS, sT=sT, sV=sV, sA=sA, sE=sE, sGx=sGx,
                    sGu=sGu, sGb=sGb, sSl=sSl, sSx=sSx)
        for _i in range(4):
            sems[f"sGs{_i}"] = sGs[_i]
        for _i in range(2):
            sems[f"sSt{_i}"] = sSt[_i]
            sems[f"sLd{_i}"] = sLd[_i]
            sems[f"sXr{_i}"] = sXr[_i]
        blk = es.enter_context(nc.Block())

        n = dict(sG=0, sS=0, sT=0, sV=0, sA=0, sE=0, sGx=0, sGu=0,
                 sGb=0, sSl=0, sSx=0,
                 sGs0=0, sGs1=0, sGs2=0, sGs3=0,
                 sSt0=0, sSt1=0, sLd0=0, sLd1=0, sXr0=0, sXr1=0)

        def barrier():
            snap = dict(n)
            allsems = tuple(sems.keys())
            for dec, eng_sems in [
                (blk.sync, allsems),
                (blk.gpsimd, allsems),
                (blk.tensor, allsems),
                (blk.vector, allsems),
                (blk.scalar, allsems),
            ]:
                def _b(eng, names=eng_sems):
                    for nm in names:
                        if snap[nm]:
                            eng.wait_ge(sems[nm], snap[nm])
                dec(_b)

        @blk.sync
        def _(sync):
            for dst, src in [(c_convw, d_in["convw"]), (c_convb, d_in["convb"]),
                             (c_dtb, d_in["dtb"]), (c_D, d_in["dvec"]),
                             (c_A, d_in["alog"]), (c_gs, d_in["gs"]),
                             (c_us, d_in["us"]), (c_ds, d_in["ds"]),
                             (c_id, d_in["ident"])]:
                sync.dma_start(out=dst[:], in_=src).then_inc(sS, 16)
                n["sS"] += 16

        @blk.scalar
        def _(sc):
            sc.wait_ge(sS, n["sS"])
            sc.activation(c_A[:], c_A[:], AF.Exp).then_inc(sA, 1)
            n["sA"] += 1

        @blk.vector
        def _(v):
            v.memset(ones_row[:], 1.0)
            v.memset(c_row4[0:1, 0:1], EPS)
            v.memset(c_row4[0:1, 1:2], -0.5)
            v.memset(c_row4[0:1, 2:3], 1.0 / DM)
            v.memset(ones_bf[:], 1.0)
            v.memset(hprev[:], 0.0)
            v.memset(hist[:], 0.0)
            v.wait_ge(sA, n["sA"])
            v.tensor_scalar(c_A[:], c_A[:], -1.0, None, ALU.mult).then_inc(sV, 1)
            n["sV"] += 1

        barrier()

        dbg_marks = {}

        for ch in range(nch):
            _sid = nc.enter_named_scope(f"A{ch}", False)[0]
            phA = "A" in phases
            Ag = phA or "g" in phases
            As = phA or "s" in phases
            At = phA or "t" in phases
            Aa = phA or "a" in phases
            Av = phA or "v" in phases
            bG, bS, bT, bV, bA = n["sG"], n["sS"], n["sT"], n["sV"], n["sA"]
            bE = n["sE"]
            bGx, bGu, bGb = n["sGx"], n["sGu"], n["sGb"]
            bSl, bSx = n["sSl"], n["sSx"]
            bGs = [n[f"sGs{i}"] for i in range(4)]
            bSt = [n[f"sSt{i}"] for i in range(2)]
            G_xbf = bGx + 16
            G_xp1 = bGu + 32
            G_slab = lambda et: (et % 4, bGs[et % 4] + 16 * (et // 4 + 1))
            S_st = lambda et: (et % 2, bSt[et % 2] + 16 * (et // 2 + 1))
            S_bc = bS + 16
            T_ip = lambda et: bT + (et + 1 if et <= 1 else 3 * et - 1)
            T_xp = lambda et: bT + (3 * et + 4 if et <= 30 else 96)
            A_ev = lambda et: bA + (et + 1 if et <= 1 else 3 * et - 1)
            A_si = lambda et: bA + 3 * et + 3
            A_ct = lambda et: bA + 3 * et + 4
            V_cv = lambda et: bV + et + 1

            @blk.gpsimd
            def _(g, ch=ch, bG=bG):
                if not Ag: return
                g.dma_start(out=xbf[:].rearrange("p (k t) -> p k t", k=NKD),
                            in_=d_in["xt_host"][ch].rearrange("k p t -> p k t")
                            ).then_inc(sGx, 16)
                g.dma_start(out=u8[:].rearrange("p (e m) -> p e m", e=NET),
                            in_=d_in["wxp1"]).then_inc(sGu, 16)
                g.dma_start(out=w_xp2[:].rearrange("p (e m) -> p e m", e=NET),
                            in_=d_in["wxp2"]).then_inc(sGu, 16)
                slots4 = [wsA[:, 0:2048], wsA[:, 2048:4096],
                          wsB[:, 0:2048], wsB[:, 2048:4096]]
                for et in range(NET):
                    if et >= 4:
                        g.wait_ge(sT, T_ip(et - 4))
                    wsl = slots4[et % 4]
                    g.dma_start(out=wsl.rearrange("p (k m) -> p k m", k=NKD),
                                in_=d_in["winx"][et]).then_inc(sGs[et % 4], 16)
                for _i in range(4):
                    n[f"sGs{_i}"] += 16 * (NET // 4)
                n["sGx"] += 16
                n["sGu"] += 32

            @blk.sync
            def _(sync, ch=ch):
                if not As: return
                for et in range(NET):
                    sync.wait_ge(sA, A_si(et))
                    xt = xt0 if et % 2 == 0 else xt1
                    sync.dma_start(out=xt_scr[et], in_=xt[:]
                                   ).then_inc(sSt[et % 2], 16)
                sync.wait_ge(sV, bV + NET + 1)
                sync.dma_start(out=bc_scr, in_=bc32[:]).then_inc(sS, 16)
                n["sSt0"] += 16 * (NET // 2)
                n["sSt1"] += 16 * (NET // 2)
                n["sS"] += 16

            @blk.tensor
            def _(t, ch=ch):
                if not At: return
                slots4 = [wsA[:, 0:2048], wsA[:, 2048:4096],
                          wsB[:, 0:2048], wsB[:, 2048:4096]]

                def ip(et):
                    sl, v_ = G_slab(et)
                    t.wait_ge(sGs[sl], v_)
                    if et >= 3:
                        t.wait_ge(sA, A_ev(et - 3))
                    wsl = slots4[et % 4]
                    px = pb[[0, 1, 4][et % 3]]
                    for k in range(NKD):
                        mm = t.matmul(px[:], wsl[:, k * P:(k + 1) * P],
                                      xbf[:, k * TC:(k + 1) * TC],
                                      start=(k == 0), stop=(k == NKD - 1))
                    mm.then_inc(sT, 1)

                def xp(et):
                    t.wait_ge(sA, A_ct(et))
                    if et >= NET - 2:
                        for _f in range(4):
                            t.matmul(pb[6][:, 0:P], ones_row[:], ones_row[:],
                                     start=True, stop=True)
                    xb = xtb0 if et % 2 == 0 else xtb1
                    t.matmul(pb[2][:], u8[:, et * P:(et + 1) * P], xb[:],
                             start=(et == 0), stop=(et == NET - 1)).then_inc(sT, 1)
                    t.matmul(pb[3][0:32, :], w_xp2[:, et * 32:(et + 1) * 32], xb[:],
                             start=(et == 0), stop=(et == NET - 1)).then_inc(sT, 1)

                t.wait_ge(sGx, G_xbf)
                t.wait_ge(sGu, G_xp1)
                ip(0)
                ip(1)
                for et in range(NET - 2):
                    xp(et)
                    ip(et + 2)
                xp(NET - 2)
                xp(NET - 1)
                for _f in range(4):
                    fm = t.matmul(pb[6][:, 0:P], ones_row[:], ones_row[:],
                                  start=True, stop=True)
                fm.then_inc(sT, 1)
                n["sT"] += 3 * NET + 1

            @blk.scalar
            def _(sc, ch=ch):
                if not Aa: return

                def evac(et):
                    sc.wait_ge(sT, T_ip(et))
                    if et >= 2:
                        sc.wait_ge(sV, V_cv(et - 2))
                        sc.wait_ge(sE, bE + 2 * et - 2)
                    cb = cb0 if et % 2 == 0 else cb1
                    px = pb[[0, 1, 4][et % 3]]
                    sc.activation(cb[:, 3:3 + TC], px[:], AF.Copy).then_inc(sA, 1)

                evac(0)
                evac(1)
                for et in range(NET):
                    sc.wait_ge(sV, V_cv(et))
                    if et >= 2:
                        _sl, _v = S_st(et - 2)
                        sc.wait_ge(sSt[_sl], _v)
                    xt = xt0 if et % 2 == 0 else xt1
                    cvout = dA0 if et % 2 == 0 else dA1
                    sc.activation(xt[:], cvout[:], AF.Silu,
                                  bias=c_convb[:, et:et + 1]).then_inc(sA, 1)
                    if et >= 1:
                        sc.wait_ge(sT, T_xp(et - 1))
                    xb = xtb0 if et % 2 == 0 else xtb1
                    sc.activation(xb[:], xt[:], AF.Copy).then_inc(sA, 1)
                    if et + 2 < NET:
                        evac(et + 2)
                    else:
                        sc.nop().then_inc(sA, 1)
                n["sA"] += 3 * NET + 2
                sc.wait_ge(sT, bT + 3 * NET + 1)
                sc.activation(dtlow[:], pb[2][:], AF.Copy).then_inc(sA, 1)
                n["sA"] += 1

            @blk.vector
            def _(v, ch=ch):
                if not Av: return
                for et in range(NET):
                    cb = cb0 if et % 2 == 0 else cb1
                    if "hc" not in SKIP:
                        v.tensor_copy(cb[:, 0:3], hist[:, et * 3:(et + 1) * 3]
                                      ).then_inc(sE, 1)
                        v.wait_ge(sE, bE + 2 * et + 1)
                    v.wait_ge(sA, A_ev(et))
                    if "cv" not in SKIP:
                        cvout = dA0 if et % 2 == 0 else dA1
                        v.tensor_scalar(t1[:], cb[:, 0:TC],
                                        c_convw[:, et * 4:et * 4 + 1], None, ALU.mult)
                        v.scalar_tensor_tensor(t2[:], cb[:, 1:1 + TC],
                                               c_convw[:, et * 4 + 1:et * 4 + 2],
                                               t1[:], ALU.mult, ALU.add)
                        v.scalar_tensor_tensor(t1[:], cb[:, 2:2 + TC],
                                               c_convw[:, et * 4 + 2:et * 4 + 3],
                                               t2[:], ALU.mult, ALU.add)
                        if et >= 2:
                            v.wait_ge(sA, A_si(et - 2))
                        v.scalar_tensor_tensor(cvout[:], cb[:, 3:3 + TC],
                                               c_convw[:, et * 4 + 3:et * 4 + 4],
                                               t1[:], ALU.mult, ALU.add).then_inc(sV, 1)
                    else:
                        v.nop().then_inc(sV, 1)
                    if "hc" not in SKIP:
                        v.tensor_copy(hist[:, et * 3:(et + 1) * 3], cb[:, TC:TC + 3]
                                      ).then_inc(sE, 1)
                    n["sE"] += 2
                n["sV"] += NET
                v.wait_ge(sT, bT + 3 * NET + 1)
                v.tensor_copy(bc32[:], pb[3][0:32, :]).then_inc(sV, 1)
                n["sV"] += 1

            barrier()

            nc.leave_named_scope(f"A{ch}", _sid, False)
            _sid = nc.enter_named_scope(f"B{ch}", False)[0]
            phB = "B" in phases
            bG, bS, bT, bV, bA = n["sG"], n["sS"], n["sT"], n["sV"], n["sA"]
            bGx, bGu, bGb = n["sGx"], n["sGu"], n["sGb"]
            bSl, bSx = n["sSl"], n["sSx"]
            bGs = [n[f"sGs{i}"] for i in range(4)]
            bLd = [n[f"sLd{i}"] for i in range(2)]
            G_wdt = bGu + 16
            G_cnm = bGb + 32
            G_zsl = lambda et: (et % 4, bGs[et % 4] + 16 * (et // 4 + 1))
            S_xt = lambda et: (et % 2, bLd[et % 2] + 16 * (et // 2 + 1))
            T_dt = lambda et: bT + 2 * et + 1
            T_z = lambda et: bT + 2 * et + 2
            A_sp = lambda et: bA + 19 * et + 1
            A_dA = lambda et, nn: bA + 19 * et + 2 + nn
            A_zr = lambda et: bA + 19 * et + 18
            A_sg = lambda et: bA + 19 * et + 19
            V_dtf = lambda et: bV + 1 if et == 0 else bV + 19 * et
            V_sc = lambda et, nn: bV + 19 * et + nn + 2
            V_zsp = lambda et: bV + 19 * et + 18
            V_y = lambda et: bV + 19 * et + 20

            @blk.gpsimd
            def _(g, ch=ch):
                if not phB: return
                g.dma_start(out=u8[:],
                            in_=d_in["wdt"].rearrange("r e m -> r (e m)")
                            ).then_inc(sGu, 16)
                g.wait_ge(sS, S_bc)
                g.dma_start(out=Bnm.rearrange("p (n t) -> p n t", n=NST),
                            in_=bc_scr[0:NST].unsqueeze(0)
                            .broadcast_to([P, NST, TC])).then_inc(sGb, 16)
                g.dma_start(out=Cnm.rearrange("p (n t) -> p n t", n=NST),
                            in_=bc_scr[NST:32].unsqueeze(0)
                            .broadcast_to([P, NST, TC])).then_inc(sGb, 16)
                slots4 = [wsA[:, 0:2048], wsA[:, 2048:4096],
                          wsB[:, 0:2048], wsB[:, 2048:4096]]
                for et in range(NET):
                    if et >= 4:
                        g.wait_ge(sT, T_z(et - 4))
                    wsl = slots4[et % 4]
                    g.dma_start(out=wsl.rearrange("p (k m) -> p k m", k=NKD),
                                in_=d_in["winz"][et]).then_inc(sGs[et % 4], 16)
                for _i in range(4):
                    n[f"sGs{_i}"] += 16 * (NET // 4)
                n["sGu"] += 16
                n["sGb"] += 32

            @blk.sync
            def _(sync, ch=ch):
                if not phB: return
                for et in range(NET):
                    if et >= 2:
                        sync.wait_ge(sV, V_y(et - 2))
                    xt = xt0 if et % 2 == 0 else xt1
                    sync.dma_start(out=xt[:], in_=xt_scr[et]
                                   ).then_inc(sLd[et % 2], 16)
                n["sLd0"] += 16 * (NET // 2)
                n["sLd1"] += 16 * (NET // 2)

            @blk.tensor
            def _(t, ch=ch):
                if not phB: return
                slots4 = [wsA[:, 0:2048], wsA[:, 2048:4096],
                          wsB[:, 0:2048], wsB[:, 2048:4096]]

                def zmm(k):
                    _sl, _v = G_zsl(k)
                    t.wait_ge(sGs[_sl], _v)
                    if k >= 2:
                        t.wait_ge(sV, V_y(k - 2))
                    wsl = slots4[k % 4]
                    pz = pb[k % 2]
                    for kk in range(NKD):
                        mm = t.matmul(pz[:], wsl[:, kk * P:(kk + 1) * P],
                                      xbf[:, kk * TC:(kk + 1) * TC],
                                      start=(kk == 0), stop=(kk == NKD - 1))
                    mm.then_inc(sT, 1)

                def dtmm(k):
                    if k >= 1:
                        t.wait_ge(sA, A_sp(k - 1))
                    t.matmul(pb[4][:], u8[:, k * P:(k + 1) * P], dtlow[:],
                             start=True, stop=True).then_inc(sT, 1)

                t.wait_ge(sGu, G_wdt)
                dtmm(0)
                zmm(0)
                for k in range(NET - 1):
                    dtmm(k + 1)
                    zmm(k + 1)
                n["sT"] += 2 * NET

            @blk.scalar
            def _(sc, ch=ch):
                if not phB: return

                def sp_block(k):
                    bcol = c_dtb[:, k:k + 1]
                    sc.wait_ge(sT, T_dt(k))
                    if k >= 1:
                        sc.wait_ge(sV, V_zsp(k - 1))
                    sc.activation(t1[:], pb[4][:], AF.Abs, bias=bcol)
                    sc.activation(t2[:], t1[:], AF.Exp, scale=-1.0)
                    sc.activation(t1[:], t2[:], AF.Ln, bias=1.0)
                    sc.activation(t2[:], pb[4][:], AF.Relu, bias=bcol
                                  ).then_inc(sA, 1)

                sp_block(0)
                for k in range(NET):
                    dtf = dt_f if k % 2 == 0 else dt_f1
                    for nn in range(NST):
                        if k == 0 and nn == 0:
                            sc.wait_ge(sV, V_dtf(0))
                        elif nn == 0:
                            sc.wait_ge(sV, V_dtf(k))
                        if nn >= 2:
                            sc.wait_ge(sV, V_sc(k, nn - 2))
                        dA = dA0 if nn % 2 == 0 else dA1
                        sc.activation(dA[:], dtf[:], AF.Exp,
                                      scale=c_A[:, k * NST + nn:k * NST + nn + 1]
                                      ).then_inc(sA, 1)
                    pz = pb[k % 2]
                    sc.wait_ge(sT, T_z(k))
                    sc.wait_ge(sV, V_dtf(k))
                    sc.activation(t1[:], pz[:], AF.Abs)
                    sc.activation(t2[:], t1[:], AF.Exp, scale=-1.0)
                    sc.activation(t1[:], t2[:], AF.Ln, bias=1.0)
                    sc.activation(t2[:], pz[:], AF.Relu, scale=-1.0).then_inc(sA, 1)
                    if k >= 1:
                        sc.wait_ge(sV, V_y(k - 1))
                    sc.wait_ge(sV, V_zsp(k))
                    sc.activation(sg[:], t1[:], AF.Exp, scale=-1.0).then_inc(sA, 1)
                    if k < NET - 1:
                        sp_block(k + 1)
                    else:
                        sc.nop().then_inc(sA, 1)
                n["sA"] += 19 * NET + 1

            @blk.vector
            def _(v, ch=ch):
                if not phB: return
                htn3 = htn.rearrange("p (t n) -> p t n", n=NST)

                def dtf_du(k):
                    dtf = dt_f if k % 2 == 0 else dt_f1
                    dup = du if k % 2 == 0 else du1
                    v.wait_ge(sA, A_sp(k))
                    v.tensor_add(dtf[:], t1[:], t2[:]).then_inc(sV, 1)
                    _sl, _v = S_xt(k)
                    v.wait_ge(sLd[_sl], _v)
                    xt = xt0 if k % 2 == 0 else xt1
                    v.tensor_mul(dup[:], dtf[:], xt[:])

                v.wait_ge(sGb, G_cnm)
                dtf_du(0)
                for k in range(NET):
                    pz = pb[k % 2]
                    dup = du if k % 2 == 0 else du1
                    xt = xt0 if k % 2 == 0 else xt1
                    for nn in range(NST):
                        dA = dA0 if nn % 2 == 0 else dA1
                        wbuf = wb0 if nn % 2 == 0 else wb1
                        v.tensor_mul(wbuf[:], dup[:],
                                     Bnm[:, nn * TC:(nn + 1) * TC])
                        v.wait_ge(sA, A_dA(k, nn))
                        v.tensor_tensor_scan(
                            htn3[:, :, nn], dA[:], wbuf[:],
                            hprev[:, k * NST + nn:k * NST + nn + 1],
                            ALU.mult, ALU.add).then_inc(sV, 1)
                    v.wait_ge(sA, A_zr(k))
                    v.tensor_add(t1[:], t1[:], t2[:]).then_inc(sV, 1)
                    if k < NET - 1:
                        dtf_du(k + 1)
                    else:
                        v.nop().then_inc(sV, 1)
                    v.wait_ge(sV, V_sc(k, NST - 1))
                    v.tensor_copy(hprev[:, k * NST:(k + 1) * NST],
                                  htn[:, (TC - 1) * NST:TC * NST])
                    v.tensor_tensor(htn3[:, :, :], htn3[:, :, :],
                                    Cnm.rearrange("p (n t) -> p t n", n=NST),
                                    ALU.mult)
                    v.tensor_reduce(yred[:], htn3[:, :, :], AX.X, ALU.add)
                    v.scalar_tensor_tensor(yred[:], xt[:], c_D[:, k:k + 1],
                                           yred[:], ALU.mult, ALU.add)
                    v.wait_ge(sA, A_sg(k))
                    v.tensor_mul(yred[:], yred[:], sg[:])
                    v.tensor_tensor(ybf[:, k * TC:(k + 1) * TC], yred[:], pz[:],
                                    ALU.mult).then_inc(sV, 1)
                n["sV"] += 19 * NET + 1

            if debug and phB:
                @blk.sync
                def _(sync, ch=ch):
                    sync.wait_ge(sV, n["sV"])
                    sync.wait_ge(sA, n["sA"])
                    sync.dma_start(out=dbg["dbg_dt"][ch], in_=dt_f1[:]).then_inc(sS, 16)
                    sync.dma_start(out=dbg["dbg_du"][ch], in_=du1[:]).then_inc(sS, 16)
                    sync.dma_start(out=dbg["dbg_yred"][ch], in_=yred[:]).then_inc(sS, 16)
                    xt = xt0 if (NET - 1) % 2 == 0 else xt1
                    sync.dma_start(out=dbg["dbg_xt"][ch], in_=xt[:]).then_inc(sS, 16)
                    sync.dma_start(out=dbg["dbg_proj"][ch], in_=bc32[:]).then_inc(sS, 16)
                    sync.dma_start(out=dbg["dbg_hprev"][ch], in_=hprev[:]).then_inc(sS, 16)
                    n["sS"] += 16 * 6

                @blk.gpsimd
                def _(g, ch=ch):
                    g.wait_ge(sA, n["sA"])
                    g.dma_start(out=dbg["dbg_dtlow"][ch], in_=dtlow[:]).then_inc(sGx, 16)
                    n["sGx"] += 16

            barrier()

            nc.leave_named_scope(f"B{ch}", _sid, False)
            _sid = nc.enter_named_scope(f"C{ch}", False)[0]
            phC = "C" in phases
            bG, bS, bT, bV, bA = n["sG"], n["sS"], n["sT"], n["sV"], n["sA"]
            bGx, bGu, bGb = n["sGx"], n["sGu"], n["sGb"]
            bSl, bSx = n["sSl"], n["sSx"]
            bGs = [n[f"sGs{i}"] for i in range(4)]
            bXr = [n[f"sXr{i}"] for i in range(2)]
            G_wo = lambda dt: (dt % 2, bGs[dt % 2] + 16 * (dt // 2 + 1))
            S_xr = lambda dt: (dt % 2, bXr[dt % 2] + 16 * (dt // 2 + 1))
            T_mm = lambda dt: bT + dt + 2 * max(0, dt - 2) + 1
            T_st = bT + 3 * NDT
            V_stt = lambda dt: bV + dt + 1
            A_c = lambda dt: bA + 2 * (dt + 1)

            @blk.gpsimd
            def _(g, ch=ch):
                if not phC: return
                for dt in range(NDT):
                    if dt >= 2:
                        g.wait_ge(sT, T_mm(dt - 2))
                    wsl = (wsA if dt % 2 == 0 else wsB)[:, 0:NET * P]
                    g.dma_start(out=wsl.rearrange("p (k m) -> p k m", k=NET),
                                in_=d_in["wout"][dt]).then_inc(sGs[dt % 2], 16)
                n["sGs0"] += 16 * (NDT // 2)
                n["sGs1"] += 16 * (NDT // 2)

            @blk.sync
            def _(sync, ch=ch):
                if not phC: return
                for dt in range(NDT):
                    if dt >= 2:
                        sync.wait_ge(sV, V_stt(dt - 2))
                    xr = xt0 if dt % 2 == 0 else xt1
                    sync.dma_start(out=xr[:], in_=d_in["xt_host"][ch][dt]
                                   ).then_inc(sXr[dt % 2], 16)
                n["sXr0"] += 16 * (NDT // 2)
                n["sXr1"] += 16 * (NDT // 2)

            @blk.tensor
            def _(t, ch=ch):
                if not phC: return
                for dt in range(NDT):
                    _sl, _v = G_wo(dt)
                    t.wait_ge(sGs[_sl], _v)
                    if dt >= 2:
                        t.wait_ge(sV, V_stt(dt - 2))
                    wsl = (wsA if dt % 2 == 0 else wsB)[:, 0:NET * P]
                    po = pb[dt % 2]
                    for k in range(NET):
                        mm = t.matmul(po[:], wsl[:, k * P:(k + 1) * P],
                                      ybf[:, k * TC:(k + 1) * TC],
                                      start=(k == 0), stop=(k == NET - 1))
                    mm.then_inc(sT, 1)
                    if dt >= 2:
                        t.wait_ge(sA, A_c(dt - 2))
                        t.matmul(pb[2][0:1, :], ones_bf[:], xtb0[:],
                                 start=(dt - 2 == 0), stop=(dt - 2 == NDT - 1)
                                 ).then_inc(sT, 1)
                        t.matmul(pb[3][0:1, :], ones_bf[:], xtb1[:],
                                 start=(dt - 2 == 0), stop=(dt - 2 == NDT - 1)
                                 ).then_inc(sT, 1)
                for dtl in range(NDT - 2, NDT):
                    t.wait_ge(sA, A_c(dtl))
                    t.matmul(pb[2][0:1, :], ones_bf[:], xtb0[:],
                             start=(dtl == 0), stop=(dtl == NDT - 1)).then_inc(sT, 1)
                    t.matmul(pb[3][0:1, :], ones_bf[:], xtb1[:],
                             start=(dtl == 0), stop=(dtl == NDT - 1)).then_inc(sT, 1)
                t.wait_ge(sA, bA + 2 * NDT + 2)
                t.wait_ge(sV, bV + NDT + 5)
                for _f in range(4):
                    t.matmul(pb[6][:, 0:P], ones_row[:], ones_row[:],
                             start=True, stop=True)
                t.matmul(pb[4][:], ones_row[:], srow_b, start=True,
                         stop=True).then_inc(sT, 1)
                t.matmul(pb[5][:], ones_row[:], trow, start=True,
                         stop=True).then_inc(sT, 1)
                n["sT"] += 3 * NDT + 2

            @blk.scalar
            def _(sc, ch=ch):
                if not phC: return
                for dt in range(NDT):
                    sc.wait_ge(sV, V_stt(dt))
                    if dt >= 1:
                        j = dt - 1
                        pos = 3 * j + 5 if j <= NDT - 3 else 3 * NDT - 2 * (NDT - 1 - j)
                        sc.wait_ge(sT, bT + pos)
                    sc.activation(xtb0[:], x_res[:, dt * TC:(dt + 1) * TC],
                                  AF.Copy).then_inc(sA, 1)
                    sc.activation(xtb1[:], x_res[:, dt * TC:(dt + 1) * TC],
                                  AF.Square).then_inc(sA, 1)
                n["sA"] += 2 * NDT
                sc.wait_ge(sV, bV + NDT + 4)
                sc.activation(trow, srow_b, AF.Ln, bias=c_row4[0:1, 0:1]).then_inc(sA, 1)
                sc.activation(srow_b, trow, AF.Exp, scale=c_row4[0:1, 1:2]).then_inc(sA, 1)
                n["sA"] += 2
                for dt in range(NDT):
                    sc.wait_ge(sV, bV + NDT + 5 + 2 * (dt + 1))
                    sc.activation(x1b[:, dt * TC:(dt + 1) * TC],
                                  x1[:, dt * TC:(dt + 1) * TC], AF.Copy
                                  ).then_inc(sA, 1)
                n["sA"] += NDT

            @blk.vector
            def _(v, ch=ch):
                if not phC: return
                for dt in range(NDT):
                    v.wait_ge(sT, T_mm(dt))
                    _sl, _v = S_xr(dt)
                    v.wait_ge(sXr[_sl], _v)
                    po = pb[dt % 2]
                    xr = xt0 if dt % 2 == 0 else xt1
                    v.scalar_tensor_tensor(x_res[:, dt * TC:(dt + 1) * TC],
                                           po[:], 1.0, xr[:], ALU.mult,
                                           ALU.add).then_inc(sV, 1)
                v.wait_ge(sT, T_st)
                v.tensor_scalar(srow_a, pb[2][0:1, :], c_row4[0:1, 2:3], None, ALU.mult).then_inc(sV, 1)
                v.tensor_scalar(srow_b, pb[3][0:1, :], c_row4[0:1, 2:3], None, ALU.mult).then_inc(sV, 1)
                v.tensor_mul(trow, srow_a, srow_a).then_inc(sV, 1)
                v.tensor_sub(srow_b, srow_b, trow).then_inc(sV, 1)
                v.wait_ge(sA, bA + 2 * NDT + 2)
                v.tensor_mul(trow, srow_a, srow_b).then_inc(sV, 1)
                n["sV"] += NDT + 5
                v.wait_ge(sT, bT + 3 * NDT + 2)
                for dt in range(NDT):
                    v.tensor_tensor(x1[:, dt * TC:(dt + 1) * TC],
                                    x_res[:, dt * TC:(dt + 1) * TC],
                                    pb[4][:], ALU.mult).then_inc(sV, 1)
                    v.tensor_sub(x1[:, dt * TC:(dt + 1) * TC],
                                 x1[:, dt * TC:(dt + 1) * TC],
                                 pb[5][:]).then_inc(sV, 1)
                n["sV"] += 2 * NDT

            if debug and phC:
                @blk.sync
                def _(sync, ch=ch):
                    sync.wait_ge(sV, n["sV"])
                    sync.dma_start(out=dbg["dbg_x1"][ch],
                                   in_=x1[:, 0:TC]).then_inc(sS, 16)
                    n["sS"] += 16

            barrier()

            nc.leave_named_scope(f"C{ch}", _sid, False)
            _sid = nc.enter_named_scope(f"D{ch}", False)[0]
            phD = "D" in phases
            bG, bS, bT, bV, bA = n["sG"], n["sS"], n["sT"], n["sV"], n["sA"]
            bGs = [n[f"sGs{i}"] for i in range(4)]
            G_g = lambda it: (it % 2, bGs[it % 2] + 16 * (it // 2 + 1))
            G_u = lambda it: (2 + it % 2, bGs[2 + it % 2] + 16 * (it // 2 + 1))
            _ng = [NIT - NIT // 2, NIT // 2]
            def G_d(dt, q):
                idx = dt * 4 + q
                s = idx % 4
                prior = (NIT // 2 + NIT % 2) if s == 0 else (NIT // 2)
                if s >= 2:
                    prior = (NIT // 2 + NIT % 2) if s == 2 else (NIT // 2)
                return (s, bGs[s] + 16 * (prior + idx // 4 + 1))
            T_g = lambda it: bT + 2 * it + 1
            T_u = lambda it: bT + 2 * it + 2
            T_dnq = lambda dt, q: bT + 2 * NIT + 4 * dt + q + 1
            T_dn = lambda dt: bT + 2 * NIT + 4 * dt + 4
            T_st2 = bT + 2 * NIT + 4 * NDT + 2 * NDT
            T_bc2 = T_st2 + 2
            T_tr = lambda dt, j: T_bc2 + 4 * dt + j + 1
            A_sil = lambda it: bA + it + 1
            V_hdn = lambda it: bV + it + 1
            V_x2 = lambda dt: bV + NIT + dt + 1
            A_cst = lambda dt: bA + NIT + 2 * (dt + 1)
            V_nr = lambda dt: bV + NIT + NDT + 5 + dt + 1
            A_ev2 = lambda dt, j: bA + NIT + 2 * NDT + 2 + 4 * dt + j + 1

            gsl = [wsA[:, 0:2048], wsB[:, 0:2048]]
            usl = [wsA[:, 2048:4096], wsB[:, 2048:4096]]
            dsl = [wsA[:, 0:1408], wsA[:, 1408:2816],
                   wsB[:, 0:1408], wsB[:, 1408:2816]]

            @blk.gpsimd
            def _(g, ch=ch):
                if not phD: return
                for it in range(NIT):
                    if it >= 2:
                        g.wait_ge(sT, T_u(it - 2))
                    g.dma_start(out=gsl[it % 2].rearrange("p (k m) -> p k m", k=NKD),
                                in_=d_in["wg"][it]).then_inc(sGs[it % 2], 16)
                    g.dma_start(out=usl[it % 2].rearrange("p (k m) -> p k m", k=NKD),
                                in_=d_in["wu"][it]).then_inc(sGs[2 + it % 2], 16)
                for dt in range(NDT):
                    for q in range(4):
                        idx = dt * 4 + q
                        if idx >= 4:
                            g.wait_ge(sT, T_dnq((idx - 4) // 4, (idx - 4) % 4))
                        elif idx == 0:
                            g.wait_ge(sT, T_u(NIT - 1))
                        g.dma_start(out=dsl[idx % 4]
                                    .rearrange("p (k m) -> p k m", k=11),
                                    in_=d_in["wd"][dt, :, q * 11:(q + 1) * 11, :]
                                    ).then_inc(sGs[idx % 4], 16)
                n["sGs0"] += 16 * (NIT - NIT // 2 + NDT)
                n["sGs1"] += 16 * (NIT // 2 + NDT)
                n["sGs2"] += 16 * (NIT - NIT // 2 + NDT)
                n["sGs3"] += 16 * (NIT // 2 + NDT)

            @blk.tensor
            def _(t, ch=ch):
                if not phD: return
                for it in range(NIT):
                    _sl, _v = G_g(it)
                    t.wait_ge(sGs[_sl], _v)
                    if it >= 2:
                        t.wait_ge(sA, A_sil(it - 2))
                        t.wait_ge(sV, V_hdn(it - 2))
                    pg = pb[it % 2]
                    for k in range(NKD):
                        mm = t.matmul(pg[:], gsl[it % 2][:, k * P:(k + 1) * P],
                                      x1b[:, k * TC:(k + 1) * TC],
                                      start=(k == 0), stop=(k == NKD - 1))
                    mm.then_inc(sT, 1)
                    _sl, _v = G_u(it)
                    t.wait_ge(sGs[_sl], _v)
                    pu = pb[2 + it % 2]
                    for k in range(NKD):
                        mm = t.matmul(pu[:], usl[it % 2][:, k * P:(k + 1) * P],
                                      x1b[:, k * TC:(k + 1) * TC],
                                      start=(k == 0), stop=(k == NKD - 1))
                    mm.then_inc(sT, 1)
                for dt in range(NDT):
                    if dt == 0:
                        t.wait_ge(sV, V_hdn(NIT - 1))
                    if dt >= 2:
                        t.wait_ge(sV, V_x2(dt - 2))
                    pd = pb[dt % 2]
                    for q in range(4):
                        idx = dt * 4 + q
                        _sl, _v = G_d(dt, q)
                        t.wait_ge(sGs[_sl], _v)
                        for kk in range(11):
                            k = q * 11 + kk
                            mm = t.matmul(pd[:], dsl[idx % 4][:, kk * P:(kk + 1) * P],
                                          hdn[:, k * TC:(k + 1) * TC],
                                          start=(k == 0), stop=(k == NIT - 1))
                        mm.then_inc(sT, 1)
                for dt in range(NDT):
                    t.wait_ge(sA, A_cst(dt))
                    t.matmul(pb[2][0:1, :], ones_bf[:], xtb0[:],
                             start=(dt == 0), stop=(dt == NDT - 1)).then_inc(sT, 1)
                    t.matmul(pb[3][0:1, :], ones_bf[:], xtb1[:],
                             start=(dt == 0), stop=(dt == NDT - 1)).then_inc(sT, 1)
                t.wait_ge(sA, bA + NIT + 2 * NDT + 2)
                t.wait_ge(sV, bV + NIT + NDT + 5)
                for _f in range(4):
                    t.matmul(pb[6][:, 0:P], ones_row[:], ones_row[:],
                             start=True, stop=True)
                t.matmul(pb[4][:], ones_row[:], srow_b, start=True,
                         stop=True).then_inc(sT, 1)
                t.matmul(pb[5][:], ones_row[:], trow, start=True,
                         stop=True).then_inc(sT, 1)
                for dt in range(NDT):
                    t.wait_ge(sV, V_nr(dt))
                    for j in range(4):
                        idx = dt * 4 + j
                        pt = pb[6 + idx % 2]
                        if idx >= 2:
                            t.wait_ge(sA, A_ev2((idx - 2) // 4, (idx - 2) % 4))
                        t.transpose(pt[:, 0:P], xt0[:, j * P:(j + 1) * P],
                                    c_id[:]).then_inc(sT, 1)
                n["sT"] += 2 * NIT + 4 * NDT + 2 * NDT + 2 + 4 * NDT

            @blk.scalar
            def _(sc, ch=ch):
                if not phD: return
                for it in range(NIT):
                    sc.wait_ge(sT, T_g(it))
                    if it >= 2:
                        sc.wait_ge(sV, V_hdn(it - 2))
                    sg = t1 if it % 2 == 0 else t2
                    sc.activation(sg[:], pb[it % 2][:], AF.Silu,
                                  scale=c_gs[:, it:it + 1]).then_inc(sA, 1)
                n["sA"] += NIT
                for dt in range(NDT):
                    sc.wait_ge(sV, V_x2(dt))
                    if dt >= 1:
                        sc.wait_ge(sT, bT + 2 * NIT + 4 * NDT + 2 * dt)
                    sc.activation(xtb0[:], x1[:, dt * TC:(dt + 1) * TC],
                                  AF.Copy).then_inc(sA, 1)
                    sc.activation(xtb1[:], x1[:, dt * TC:(dt + 1) * TC],
                                  AF.Square).then_inc(sA, 1)
                n["sA"] += 2 * NDT
                sc.wait_ge(sV, bV + NIT + NDT + 4)
                sc.activation(trow, srow_b, AF.Ln, bias=c_row4[0:1, 0:1]).then_inc(sA, 1)
                sc.activation(srow_b, trow, AF.Exp, scale=c_row4[0:1, 1:2]).then_inc(sA, 1)
                n["sA"] += 2
                for dt in range(NDT):
                    for j in range(4):
                        idx = dt * 4 + j
                        sc.wait_ge(sT, T_tr(dt, j))
                        pt = pb[6 + idx % 2]
                        sc.activation(obufs[j][:, dt * P:(dt + 1) * P],
                                      pt[:, 0:P], AF.Copy).then_inc(sA, 1)
                n["sA"] += 4 * NDT

            @blk.vector
            def _(v, ch=ch):
                if not phD: return
                for it in range(NIT):
                    v.wait_ge(sT, T_u(it))
                    v.wait_ge(sA, A_sil(it))
                    sg = t1 if it % 2 == 0 else t2
                    v.scalar_tensor_tensor(hdn[:, it * TC:(it + 1) * TC],
                                           pb[2 + it % 2][:], c_us[:, it:it + 1],
                                           sg[:], ALU.mult, ALU.mult
                                           ).then_inc(sV, 1)
                for dt in range(NDT):
                    v.wait_ge(sT, T_dn(dt))
                    v.scalar_tensor_tensor(x1[:, dt * TC:(dt + 1) * TC],
                                           pb[dt % 2][:], c_ds[:, dt:dt + 1],
                                           x1[:, dt * TC:(dt + 1) * TC],
                                           ALU.mult, ALU.add).then_inc(sV, 1)
                v.wait_ge(sT, T_st2)
                v.tensor_scalar(srow_a, pb[2][0:1, :], c_row4[0:1, 2:3], None, ALU.mult).then_inc(sV, 1)
                v.tensor_scalar(srow_b, pb[3][0:1, :], c_row4[0:1, 2:3], None, ALU.mult).then_inc(sV, 1)
                v.tensor_mul(trow, srow_a, srow_a).then_inc(sV, 1)
                v.tensor_sub(srow_b, srow_b, trow).then_inc(sV, 1)
                v.wait_ge(sA, bA + NIT + 2 * NDT + 2)
                v.tensor_mul(trow, srow_a, srow_b).then_inc(sV, 1)
                n["sV"] += NIT + NDT + 5
                v.wait_ge(sT, T_bc2)
                for dt in range(NDT):
                    if dt >= 1:
                        v.wait_ge(sT, T_tr(dt - 1, 3))
                    v.tensor_tensor(xt0[:], x1[:, dt * TC:(dt + 1) * TC],
                                    pb[4][:], ALU.mult)
                    v.tensor_sub(xt0[:], xt0[:], pb[5][:]).then_inc(sV, 1)
                n["sV"] += NDT

            @blk.sync
            def _(sync, ch=ch):
                if not phD: return
                sync.wait_ge(sA, bA + NIT + 2 * NDT + 2 + 4 * NDT)
                for j in range(4):
                    sync.dma_start(out=out_d[ch * TC + j * P: ch * TC + (j + 1) * P, :],
                                   in_=obufs[j][:]).then_inc(sS, 16)
                n["sS"] += 64

            if debug and phD:
                @blk.sync
                def _(sync, ch=ch):
                    sync.wait_ge(sV, n["sV"])
                    sync.dma_start(out=dbg["dbg_y"][ch],
                                   in_=x1[:, 0:TC]).then_inc(sS, 16)
                    n["sS"] += 16

            barrier()

            nc.leave_named_scope(f"D{ch}", _sid, False)
        @blk.gpsimd
        def _(g):
            g.wait_ge(sS, n["sS"])

        @blk.sync
        def _(sync):
            sync.wait_ge(sS, n["sS"])

    return nc


_PROG_CACHE = {}


def _get_program(debug=False):
    key = bool(debug)
    if key not in _PROG_CACHE:
        _PROG_CACHE[key] = build_program(debug=debug)
    return _PROG_CACHE[key]


def kernel(**inputs):
    w = _host_prep(inputs)
    x = np.asarray(inputs["L0_attn_hidden_states"], np.float32)
    nc = _get_program(debug=False)
    in_maps = []
    for c in range(8):
        m = dict(w)
        m["xt_host"] = _host_x(x[c])
        in_maps.append(m)
    res = run_bass_kernel_spmd(nc, in_maps, core_ids=list(range(8)))
    return np.stack([r["out"] for r in res.results], axis=0)
